# version 13
# baseline (speedup 1.0000x reference)
import contextlib

import numpy as np
import ml_dtypes

import concourse.bass as bass
import concourse.mybir as mybir
from concourse.bass_utils import run_bass_kernel_spmd

F32 = mybir.dt.float32
BF16 = mybir.dt.bfloat16
I32 = mybir.dt.int32
U32 = mybir.dt.uint32
AF = mybir.ActivationFunctionType
ALU = mybir.AluOpType
AX = mybir.AxisListType
BF16NP = ml_dtypes.bfloat16

L = 4096
NCORES = 8
KC = 17
TAU = 2560
I_DIM = 11008
IC_H = 43
DC = 32
EPS = 1e-5


class Prog:
    ENGINES = ("tensor", "vector", "scalar", "gpsimd", "sync")

    def __init__(self, nc, stack):
        self.nc = nc
        self.stack = stack
        self.streams = {e: [] for e in self.ENGINES}
        self.cnt = {}
        self.esem = {e: None for e in self.ENGINES}
        for e in self.ENGINES:
            self.esem[e] = self.sem(f"ms_{e}")

    def sem(self, name):
        h = self.stack.enter_context(self.nc.semaphore(name))
        self.cnt[h] = 0
        return h

    def emit(self, engine, fn, waits=(), inc=None, inc_by=1):
        waits = tuple((s, v) for (s, v) in waits if v > 0)
        for s, v in waits:
            assert v <= self.cnt[s], f"wait beyond emitted on {s.name}: {v} > {self.cnt[s]}"
        if inc is True:
            inc = self.esem[engine]

        def op(eng, fn=fn, waits=waits, inc=inc, inc_by=inc_by):
            for (s, v) in waits:
                eng.wait_ge(s, v)
            inst = fn(eng)
            if inc is not None:
                inst.then_inc(inc, inc_by)
        self.streams[engine].append(op)
        if inc is not None:
            self.cnt[inc] += inc_by
            return self.cnt[inc]
        return None

    def milestone(self, engine, waits=()):
        return self.emit(engine, lambda e: e.drain(), waits=waits, inc=True)

    def dma(self, q, out, in_, sem, waits=(), **kw):
        return self.emit(q, lambda eng: eng.dma_start(out=out, in_=in_, **kw),
                         waits=waits, inc=sem, inc_by=16)

    def wait_only(self, engine, waits):
        waits = tuple((s, v) for (s, v) in waits if v > 0)
        for s, v in waits:
            assert v <= self.cnt[s]

        def op(eng, waits=waits):
            for (s, v) in waits:
                eng.wait_ge(s, v)
        self.streams[engine].append(op)

    def join_all(self, engines=None):
        snap = [(s, v) for s, v in self.cnt.items() if v > 0]
        for e in (engines or self.ENGINES):
            self.wait_only(e, snap)

    def finalize(self):
        nc = self.nc
        with nc.Block() as block:
            for e in self.ENGINES:
                fns = self.streams[e]
                if not fns:
                    continue
                def body(eng, fns=fns):
                    for f in fns:
                        f(eng)
                getattr(block, e)(body)


def build_program(R):
    assert R % 512 == 0
    NT = R // 128
    TPS = 4
    NSUB = NT // TPS
    TS = TPS * 128
    TM = 512
    TH = R // TM

    nc = bass.Bass("TRN2", target_bir_lowering=False, debug=False)
    stack = contextlib.ExitStack()
    p = Prog(nc, stack)

    x_in = nc.dram_tensor("x", [R, L], F32, kind="ExternalInput").ap()
    gw = nc.dram_tensor("gw", [2, IC_H, 128, DC, 128], I32, kind="ExternalInput").ap()
    uw = nc.dram_tensor("uw", [2, IC_H, 128, DC, 128], I32, kind="ExternalInput").ap()
    dw = nc.dram_tensor("dw", [2, DC, 128, IC_H, 128], I32, kind="ExternalInput").ap()
    csq = nc.dram_tensor("csq", [KC, 4, 128, 2048], BF16, kind="ExternalInput").ap()
    dmat = nc.dram_tensor("dmat", [KC, 128, TAU], BF16, kind="ExternalInput").ap()
    scl = nc.dram_tensor("scl", [128, 4], F32, kind="ExternalInput").ap()
    idbf = nc.dram_tensor("idbf", [128, 128], BF16, kind="ExternalInput").ap()
    idf = nc.dram_tensor("idf", [128, 128], F32, kind="ExternalInput").ap()
    out = nc.dram_tensor("out", [R, L], F32, kind="ExternalOutput").ap()
    xnd = nc.dram_tensor("xnd", [2 * R, L], BF16, kind="Internal").ap()
    xnd_w = xnd.rearrange("(r two) c -> r (two c)", two=2)

    sb = lambda n, s, d: stack.enter_context(nc.sbuf_tensor(n, s, d))

    t_xnT = sb("xnT", [128, DC * R], BF16)
    t_scl = sb("t_scl", [128, 4], F32)
    t_idb = sb("t_idb", [128, 128], BF16)
    t_idf = sb("t_idf", [128, 128], F32)
    t_eps = sb("t_eps", [128, 1], F32)
    gs_ap = t_scl[:, 0:1]
    us_ap = t_scl[:, 1:2]
    ds_ap = t_scl[:, 2:3]

    sSC = p.sem("sSC")
    sX = [p.sem("sX0"), p.sem("sX1")]
    sXND = [p.sem("sXND0"), p.sem("sXND1")]
    sO0 = p.sem("sO0")
    sAG = p.sem("sAG")
    sCQ = [p.sem(f"sCQ{i}") for i in range(4)]
    sDQ = [p.sem(f"sDQ{i}") for i in range(4)]
    sGA = [p.sem("sGA0"), p.sem("sGA1")]
    sGW = [p.sem("sGW0"), p.sem("sGW1")]
    sUW = [p.sem("sUW0"), p.sem("sUW1")]
    sDW = [p.sem("sDW0"), p.sem("sDW1")]
    sDO = [p.sem("sDO0"), p.sem("sDO1")]

    T, V, A, G = (p.esem["tensor"], p.esem["vector"], p.esem["scalar"], p.esem["gpsimd"])

    p.dma("sync", t_scl[:], scl[:], sSC)
    p.dma("sync", t_idb[:], idbf[:], sSC)
    p.dma("sync", t_idf[:], idf[:], sSC)
    c_pro = p.cnt[sSC]
    o0_done = p.dma("scalar", out[:, :], x_in[:, :], sO0)
    v_eps = p.emit("vector", lambda e: e.memset(t_eps[:], EPS), inc=True)

    with contextlib.ExitStack() as attn_sb:
        asb = lambda n, s, d: attn_sb.enter_context(nc.sbuf_tensor(n, s, d))
        t_x = [asb("t_x0", [128, L], F32), asb("t_x1", [128, L], F32)]
        t_xn = asb("t_xn", [128, L], BF16)
        t_stats = asb("t_stats", [128, 48], F32)
        t_mv = asb("t_mv", [128, 2], F32)
        t_sd = asb("t_sd", [128, 1], F32)
        t_rstd = asb("t_rstd", [128, 1], F32)
        t_cq = [asb(f"t_cq{i}", [128, 2048], BF16) for i in range(4)]
        t_dq = [asb(f"t_dq{i}", [128, 1024], BF16) for i in range(4)]
        t_pa = asb("t_pa", [128, TM], F32)
        t_pb = asb("t_pb", [128, TM], F32)
        t_P = asb("t_P", [128, KC * TM], BF16)
        t_corr = [asb(f"t_corr{i}", [128, TAU], BF16) for i in range(TPS)]
        t_max8 = asb("t_max8", [128, 8], F32)
        t_sub8 = asb("t_sub8", [128, 8], F32)
        t_exp8 = asb("t_exp8", [128, 8], F32)
        t_ssum = asb("t_ssum", [128, 1], F32)
        t_rs = asb("t_rs", [128, 1], F32)
        t_w8A = asb("t_w8A", [128, 8 * TPS], F32)
        t_idx8 = asb("t_idx8", [128, 8], U32)
        t_b8A = asb("t_b8A", [128, 8 * NT], U32)
        t_gidxA = asb("t_gidxA", [128, 8 * TPS], U32)
        t_gath = [asb("t_g0", [128, L], BF16), asb("t_g1", [128, L], BF16)]
        t_agg = asb("t_agg", [128, L], BF16)

        st = {
            "x_freeV": [0, 0], "mv_freeA": 0, "v_mv": 0, "tt_last": 0,
            "cq_freeT": [0, 0, 0, 0], "dq_freeT": [0, 0, 0, 0], "ga_freeV": [0, 0],
            "agg_freeG": 0, "xnd_done": {}, "corr_evacV": {}, "tp_freeA_last": 0,
            "fence_ln": None, "fence_corr": None, "v_gidx": 0,
        }
        for gt_ in range(NT):
            g_iota_all = p.emit("gpsimd", lambda e, gt_=gt_: e.iota(
                t_b8A[:, gt_ * 8:(gt_ + 1) * 8], pattern=[[0, 8]],
                base=gt_ * 128 * 2 * L, channel_multiplier=2 * L), inc=True)

        def emit_ln_dft(s_ix, ps):
            aps = lambda n, s, d: ps.enter_context(nc.psum_tensor(n, s, d))
            tp = [aps(f"tp0_{s_ix}", [128, 1024], BF16), aps(f"tp1_{s_ix}", [128, 1024], BF16)]
            fre = [aps(f"fre0_{s_ix}", [128, TM], F32), aps(f"fre1_{s_ix}", [128, TM], F32)]
            fim = [aps(f"fim0_{s_ix}", [128, TM], F32), aps(f"fim1_{s_ix}", [128, TM], F32)]
            fence = st["fence_ln"] or [(V, p.cnt[V]), (A, p.cnt[A])]
            p.wait_only("tensor", fence)

            tp_freeA = [0, 0]
            ln_v = 0
            for g in range(TPS):
                gt = s_ix * TPS + g
                xs = gt % 2
                c_x = p.dma("sync", t_x[xs][:], x_in[gt * 128:(gt + 1) * 128, :], sX[xs],
                            waits=[(V, st["x_freeV"][xs])])
                for kk in range(8):
                    p.emit("vector", lambda e, kk=kk, xs=xs: e.bn_stats(
                        t_stats[:, 6 * kk:6 * kk + 6], t_x[xs][:, 512 * kk:512 * (kk + 1)]),
                        waits=([(sX[xs], c_x), (V, st["v_mv"])] if kk == 0 else []))
                v_s = p.milestone("vector")
                st["v_mv"] = p.emit("vector", lambda e: e.bn_aggr(t_mv[:], t_stats[:]),
                                    waits=[(V, v_s), (A, st["mv_freeA"])], inc=True)
                st["mv_freeA"] = p.emit("scalar", lambda e: e.activation(
                    t_sd[:], t_mv[:, 1:2], AF.Sqrt, bias=t_eps[:]),
                    waits=[(V, st["v_mv"]), (V, v_eps)], inc=True)
                v_rec = p.emit("vector", lambda e: e.reciprocal(t_rstd[:], t_sd[:]),
                               waits=[(A, st["mv_freeA"])], inc=True)
                ln_v = p.emit("vector", lambda e, xs=xs: e.tensor_scalar(
                    t_xn[:], t_x[xs][:], t_mv[:, 0:1], t_rstd[:, 0:1],
                    ALU.subtract, ALU.mult),
                    waits=[(V, v_rec), (T, st["tt_last"]),
                           (sXND[0], p.cnt[sXND[0]]), (sXND[1], p.cnt[sXND[1]])],
                    inc=True)
                st["x_freeV"][xs] = ln_v
                p.dma("scalar", xnd_w[gt * 128:(gt + 1) * 128, 0:L], t_xn[:], sXND[s_ix % 2],
                      waits=[(V, ln_v)])
                p.dma("scalar", xnd_w[gt * 128:(gt + 1) * 128, L:2 * L], t_xn[:], sXND[s_ix % 2])
                for lc in range(DC):
                    tw = [(V, ln_v), (A, tp_freeA[lc % 2])]
                    if gt == 0 and lc == 0:
                        tw.append((sSC, c_pro))
                    st["tt_last"] = p.emit("tensor", lambda e, lc=lc: e.transpose(
                        tp[lc % 2][:, 0:128], t_xn[:, lc * 128:(lc + 1) * 128],
                        t_idb[:]), waits=tw, inc=True)
                    tp_freeA[lc % 2] = p.emit("scalar", lambda e, lc=lc, gt=gt: e.activation(
                        t_xnT[:, lc * R + gt * 128: lc * R + (gt + 1) * 128],
                        tp[lc % 2][:, 0:128], AF.Copy),
                        waits=[(T, st["tt_last"])], inc=True)
            st["xnd_done"][s_ix] = (sXND[s_ix % 2], p.cnt[sXND[s_ix % 2]])
            st["tp_freeA_last"] = max(tp_freeA)

            fe_freeA = [0, 0]
            pa_freeG = 0
            qmile = st["cq_freeT"]
            qtoks = {}
            for q in range(4):
                qtoks[(0, q)] = p.dma("scalar", t_cq[q][:], csq[0, q], sCQ[q],
                                      waits=[(T, qmile[q])])
            for kc in range(KC):
                for lc in range(DC):
                    q, m = lc // 8, lc % 8
                    w = []
                    if m == 0:
                        w.append((sCQ[q], qtoks[(kc, q)]))
                        if lc == 0:
                            w.append((A, st["tp_freeA_last"]))
                            w.append((A, fe_freeA[kc % 2]))
                    rhs = t_xnT[:, bass.ts(lc, R)][:, s_ix * TS: s_ix * TS + TM]
                    p.emit("tensor", lambda e, kc=kc, q=q, m=m, rhs=rhs, lc=lc: e.matmul(
                        fre[kc % 2][:], t_cq[q][:, m * 128:(m + 1) * 128], rhs,
                        start=(lc == 0), stop=(lc == DC - 1)), waits=w)
                    p.emit("tensor", lambda e, kc=kc, q=q, m=m, rhs=rhs, lc=lc: e.matmul(
                        fim[kc % 2][:], t_cq[q][:, 1024 + m * 128:1024 + (m + 1) * 128],
                        rhs, start=(lc == 0), stop=(lc == DC - 1)))
                    if m == 7:
                        qmile[q] = p.milestone("tensor")
                t_kc = qmile[3]
                if kc + 1 < KC:
                    for q in range(4):
                        qtoks[(kc + 1, q)] = p.dma("scalar", t_cq[q][:], csq[kc + 1, q],
                                                   sCQ[q], waits=[(T, qmile[q])])
                p.emit("scalar", lambda e, kc=kc: e.activation(
                    t_pa[:], fre[kc % 2][:], AF.Square),
                    waits=[(T, t_kc), (G, pa_freeG)])
                fe_freeA[kc % 2] = p.emit("scalar", lambda e, kc=kc: e.activation(
                    t_pb[:], fim[kc % 2][:], AF.Square), inc=True)
                pa_freeG = p.emit("gpsimd", lambda e, kc=kc: e.tensor_add(
                    t_P[:, kc * TM:(kc + 1) * TM], t_pa[:], t_pb[:]),
                    waits=[(A, fe_freeA[kc % 2])], inc=True)
            st["v_P"] = pa_freeG
            st["fence_corr"] = [(V, p.cnt[V]), (A, p.cnt[A])]

        def emit_corr(s_ix):
            with contextlib.ExitStack() as ps_b:
                cor = [ps_b.enter_context(nc.psum_tensor(f"cor{i}_{s_ix}", [128, 512], F32))
                       for i in range(8)]
                p.wait_only("tensor", st["fence_corr"])
                sched = [(tq, kc) for tq in range(3) for kc in range(KC)]
                dmile = st["dq_freeT"]
                dtoks = {}

                def emit_d(i):
                    tq, kc = sched[i]
                    slot = i % 4
                    nblk = 2 if tq < 2 else 1
                    dtoks[i] = p.dma("scalar", t_dq[slot][:, 0:512 * nblk],
                                     dmat[kc, :, tq * 1024:tq * 1024 + 512 * nblk],
                                     sDQ[slot], waits=[(T, dmile[slot])])

                for i in range(3):
                    emit_d(i)
                corr_evacA = 0
                for i, (tq, kc) in enumerate(sched):
                    slot = i % 4
                    nblk = 2 if tq < 2 else 1
                    if i + 3 < len(sched):
                        emit_d(i + 3)
                    w = [(sDQ[slot], dtoks[i])]
                    if kc == 0:
                        w.append((G, st["v_P"]))
                        if corr_evacA > 0:
                            w.append((A, corr_evacA))
                    for t in range(TPS):
                        for blk in range(nblk):
                            p.emit("tensor", lambda e, kc=kc, t=t, blk=blk, slot=slot: e.matmul(
                                cor[2 * t + blk][:],
                                t_P[:, kc * TM + t * 128: kc * TM + (t + 1) * 128],
                                t_dq[slot][:, blk * 512:(blk + 1) * 512],
                                start=(kc == 0), stop=(kc == KC - 1)), waits=w)
                            w = []
                    dmile[slot] = p.milestone("tensor")
                    if kc == KC - 1:
                        t_cordone = dmile[slot]
                        for t in range(TPS):
                            for blk in range(nblk):
                                ew = [(T, t_cordone)]
                                if tq == 0 and t == 0 and blk == 0:
                                    ew.append((V, st.get("tail_corr_readV", 0)))
                                corr_evacA = p.emit("scalar", lambda e, t=t, blk=blk, tq=tq: e.activation(
                                    t_corr[t][:, tq * 1024 + blk * 512: tq * 1024 + (blk + 1) * 512],
                                    cor[2 * t + blk][:], AF.Copy), waits=ew, inc=True)
                st["corr_evacV"][s_ix] = corr_evacA
                st["fence_ln"] = [(V, p.cnt[V]), (A, p.cnt[A])]

        def emit_tail_p1(s_ix):
            corr_evacV = st["corr_evacV"][s_ix]
            for g in range(TPS):
                gt = s_ix * TPS + g
                vm = p.emit("vector", lambda e, g=g: e.max(t_max8[:], t_corr[g][:]),
                            waits=[(A, corr_evacV), (V, p.cnt[V])], inc=True)
                p.wait_only("vector", [(V, vm)])
                vi = p.emit("vector", lambda e, g=g: e.max_index(
                    t_idx8[:], t_max8[:], t_corr[g][:]), inc=True)
                p.wait_only("vector", [(V, vi)])
                v_sub = p.emit("vector", lambda e: e.tensor_scalar(
                    t_sub8[:], t_max8[:], t_max8[:, 0:1], None, ALU.subtract),
                    inc=True)
                a_exp = p.emit("scalar", lambda e: e.activation(
                    t_exp8[:], t_sub8[:], AF.Exp), waits=[(V, v_sub), (A, p.cnt[A])],
                    inc=True)
                v_red = p.emit("vector", lambda e: e.reduce_sum(t_ssum[:], t_exp8[:], AX.X),
                               waits=[(A, a_exp)], inc=True)
                v_rs = p.emit("vector", lambda e: e.reciprocal(t_rs[:], t_ssum[:]),
                              waits=[(V, v_red)], inc=True)
                p.emit("vector", lambda e, g=g: e.tensor_scalar(
                    t_w8A[:, g * 8:(g + 1) * 8], t_exp8[:], t_rs[:, 0:1], None, ALU.mult),
                    waits=[(V, v_rs)], inc=True)
                st["v_gidx"] = p.emit("vector", lambda e, g=g, gt=gt: e.tensor_tensor(
                    t_gidxA[:, g * 8:(g + 1) * 8], t_b8A[:, gt * 8:(gt + 1) * 8],
                    t_idx8[:], ALU.add),
                    waits=[(G, g_iota_all)], inc=True)
                st["tail_corr_readV"] = vi
            st["v_w8all"] = st["v_gidx"]

        def emit_tail_p2(s_ix):
            v_w8all = st["v_w8all"]
            xnd_done = st["xnd_done"][s_ix]
            for g in range(TPS):
                gt = s_ix * TPS + g
                v_agg = 0
                for k in range(8):
                    slot = k % 2
                    gtok = p.emit("gpsimd", lambda e, g=g, k=k, slot=slot: e.indirect_dma_start(
                        t_gath[slot][:], None, xnd[:, :],
                        bass.IndirectOffsetOnAxis(ap=t_gidxA[:, g * 8 + k:g * 8 + k + 1],
                                                  axis=1)),
                        waits=[(V, v_w8all), xnd_done, (V, st["ga_freeV"][slot])],
                        inc=sGA[slot], inc_by=16)
                    if k == 0:
                        st["ga_freeV"][slot] = p.emit(
                            "vector", lambda e, g=g, slot=slot: e.tensor_scalar(
                                t_agg[:], t_gath[slot][:], t_w8A[:, g * 8:g * 8 + 1],
                                None, ALU.mult),
                            waits=[(sGA[slot], gtok), (sAG, p.cnt[sAG])], inc=True)
                    else:
                        st["ga_freeV"][slot] = p.emit(
                            "vector", lambda e, g=g, k=k, slot=slot: e.scalar_tensor_tensor(
                                t_agg[:], t_gath[slot][:], t_w8A[:, g * 8 + k:g * 8 + k + 1],
                                t_agg[:], ALU.mult, ALU.add),
                            waits=[(sGA[slot], gtok), (V, v_agg)], inc=True)
                    v_agg = st["ga_freeV"][slot]
                for half in range(2):
                    p.dma("gpsimd", out[gt * 128:(gt + 1) * 128, half * 2048:(half + 1) * 2048],
                          t_agg[:, half * 2048:(half + 1) * 2048], sAG,
                          waits=[(V, v_agg), (sO0, o0_done)], accum_op=ALU.add)

        scopes = {}
        for s_ix in range(NSUB):
            scopes[s_ix] = contextlib.ExitStack()
            emit_ln_dft(s_ix, scopes[s_ix])
            if s_ix > 0:
                emit_tail_p1(s_ix - 1)
                emit_tail_p2(s_ix - 1)
            scopes[s_ix].close()
            emit_corr(s_ix)
        emit_tail_p1(NSUB - 1)
        ffn_post_cb = lambda: emit_tail_p2(NSUB - 1)

    early_fence = [(V, max(st["x_freeV"])),
                   (sX[0], p.cnt[sX[0]]), (sX[1], p.cnt[sX[1]]), (sSC, c_pro)]
    with contextlib.ExitStack() as ffn_sb:
        fsb = lambda n, s, d: ffn_sb.enter_context(nc.sbuf_tensor(n, s, d))
        t_war = fsb("war", [128, 4 * DC * 128], BF16)
        t_gwb = [t_war[:, 0:DC * 128], t_war[:, DC * 128:2 * DC * 128]]
        t_uwb = [t_war[:, 2 * DC * 128:3 * DC * 128], t_war[:, 3 * DC * 128:4 * DC * 128]]
        t_dwb = [t_war[:, 0:IC_H * 128], t_war[:, IC_H * 128:2 * IC_H * 128]]
        t_silu = [fsb("sil0", [128, TM], F32), fsb("sil1", [128, TM], F32)]
        t_dtmp = [fsb("dt0", [128, TM], F32), fsb("dt1", [128, TM], F32)]
        t_dacc = [fsb(f"da{th}", [128, 2048], BF16) for th in range(TH)]
        t_mid = fsb("mid", [128, IC_H * R], BF16)

        gw_freeT = [0, 0]
        dw_freeT = [0, 0]
        do_uses = [0, 0]
        silu_freeV = [0, 0]
        dacc_lastV = [0, 0]
        t_dn = 0
        out_view4 = out.rearrange("(th tq pp) (dc4 c) -> th pp tq dc4 c",
                                  th=TH, tq=4, pp=128, c=512)

        for h in range(2):
            with contextlib.ExitStack() as ps_c:
                pg = [[ps_c.enter_context(nc.psum_tensor(f"pg{par}{th}_{h}", [128, TM], F32))
                       for th in range(TH)] for par in range(2)]
                pu = [[ps_c.enter_context(nc.psum_tensor(f"pu{par}{th}_{h}", [128, TM], F32))
                       for th in range(TH)] for par in range(2)]
                if h == 0:
                    p.wait_only("tensor", st["fence_ln"])
                else:
                    p.wait_only("tensor", [(V, p.cnt[V]), (A, p.cnt[A])])
                mid_freeV = [0, 0]
                for ic in range(IC_H):
                    par = ic % 2
                    xw = [(T, gw_freeT[par])]
                    if ic < 2 and h == 0:
                        xw = list(early_fence)
                    if ic < 2 and h == 1:
                        xw = [(T, t_dn),
                              (sDW[0], p.cnt[sDW[0]]), (sDW[1], p.cnt[sDW[1]])]
                    gtok = p.dma("gpsimd", t_gwb[par][:], gw[h, ic], sGW[par], waits=xw)
                    utok = p.dma("gpsimd", t_uwb[par][:], uw[h, ic], sUW[par])
                    if h == 0 and ic == 3 and ffn_post_cb is not None:
                        ffn_post_cb()
                        ffn_post_cb = None
                    w = [(sGW[par], gtok), (sUW[par], utok), (V, mid_freeV[par])]
                    for dc2 in range(DC):
                        for th in range(TH):
                            p.emit("tensor", lambda e, par=par, th=th, dc2=dc2: e.matmul(
                                pg[par][th][:], t_gwb[par][:, dc2 * 128:(dc2 + 1) * 128],
                                t_xnT[:, dc2 * R + th * TM: dc2 * R + (th + 1) * TM],
                                start=(dc2 == 0), stop=(dc2 == DC - 1)), waits=w)
                            w = []
                        for th in range(TH):
                            p.emit("tensor", lambda e, par=par, th=th, dc2=dc2: e.matmul(
                                pu[par][th][:], t_uwb[par][:, dc2 * 128:(dc2 + 1) * 128],
                                t_xnT[:, dc2 * R + th * TM: dc2 * R + (th + 1) * TM],
                                start=(dc2 == 0), stop=(dc2 == DC - 1)))
                    t_gu = p.milestone("tensor")
                    gw_freeT[par] = t_gu
                    for th in range(TH):
                        sw = [(T, t_gu), (sSC, c_pro), (V, silu_freeV[th])]
                        if h == 0 and ic < 2:
                            sw += [(sXND[0], p.cnt[sXND[0]]), (sXND[1], p.cnt[sXND[1]])]
                        a_si = p.emit("scalar", lambda e, par=par, th=th: e.activation(
                            t_silu[th][:], pg[par][th][:], AF.Silu, scale=gs_ap),
                            waits=sw, inc=True)
                        mw = [(A, a_si)]
                        if h == 0 and ic == 14 and th == 0:
                            mw = mw + [(V, p.cnt[V]), (G, p.cnt[G]),
                                       (sAG, p.cnt[sAG]),
                                       (sGA[0], p.cnt[sGA[0]]),
                                       (sGA[1], p.cnt[sGA[1]])]
                        mid_freeV[par] = silu_freeV[th] = p.emit(
                            "vector",
                            lambda e, par=par, th=th, ic=ic: e.scalar_tensor_tensor(
                                t_mid[:, ic * R + th * TM: ic * R + (th + 1) * TM],
                                pu[par][th][:], us_ap, t_silu[th][:], ALU.mult, ALU.mult),
                            waits=mw, inc=True)

            with contextlib.ExitStack() as ps_d:
                pd = [[ps_d.enter_context(nc.psum_tensor(f"pd{par}{th}_{h}", [128, TM], F32))
                       for th in range(TH)] for par in range(2)]
                pt = [ps_d.enter_context(nc.psum_tensor(f"pt{par}_{h}", [128, 512], F32))
                      for par in range(2)]
                p.wait_only("tensor", [(V, p.cnt[V]), (A, p.cnt[A])])
                ag_total = p.cnt[sAG]
                dtmp_freeT = [0, 0]
                pd_freeV = [0, 0]
                pt_freeV = [0, 0]
                t_dns = {}

                def down_epilogue(dc):
                    par = dc % 2
                    dcm = dc % 4
                    for th in range(TH):
                        tpar = (dc * TH + th) % 2
                        pd_freeV[par] = p.emit(
                            "vector", lambda e, par=par, th=th, tpar=tpar: e.tensor_scalar(
                                t_dtmp[tpar][:], pd[par][th][:], ds_ap, None, ALU.mult),
                            waits=[(T, t_dns[dc]), (T, dtmp_freeT[tpar])], inc=True)
                        w = [(V, pd_freeV[par]), (V, pt_freeV[tpar])]
                        for tq in range(4):
                            dtmp_freeT[tpar] = p.emit(
                                "tensor", lambda e, tq=tq, tpar=tpar: e.transpose(
                                    pt[tpar][:, tq * 128:(tq + 1) * 128],
                                    t_dtmp[tpar][:, tq * 128:(tq + 1) * 128], t_idf[:]),
                                waits=w, inc=True)
                            w = []
                        ew = [(T, dtmp_freeT[tpar]), (V, pt_freeV[tpar])]
                        if dcm == 0:
                            ew.append((sDO[th], 16 * do_uses[th]))
                        pt_freeV[tpar] = dacc_lastV[th] = p.emit(
                            "vector", lambda e, th=th, tpar=tpar, dcm=dcm: e.tensor_copy(
                                t_dacc[th].rearrange("p (tq dd j) -> p tq dd j",
                                                     tq=4, dd=4)[:, :, dcm, :],
                                pt[tpar].rearrange("p (tq j) -> p tq j", tq=4)),
                            waits=ew, inc=True)
                        if dcm == 3:
                            do_uses[th] += 1
                            p.emit("gpsimd", lambda e, th=th, dc=dc: e.dma_start(
                                out=out_view4[th, :, :, dc // 4, :],
                                in_=t_dacc[th].rearrange("p (tq c) -> p tq c", tq=4),
                                accum_op=ALU.add),
                                waits=[(V, dacc_lastV[th]), (sO0, o0_done),
                                       (sAG, ag_total)],
                                inc=sDO[th], inc_by=16)

                dtoks = {}

                def dw_slab(dc):
                    par = dc % 2
                    xw = [(T, dw_freeT[par])]
                    if dc < 2:
                        xw = [(T, gw_freeT[0]), (T, gw_freeT[1]),
                              (sGW[0], p.cnt[sGW[0]]), (sGW[1], p.cnt[sGW[1]]),
                              (sUW[0], p.cnt[sUW[0]]), (sUW[1], p.cnt[sUW[1]]),
                              (V, p.cnt[V]), (A, p.cnt[A])]
                    dtoks[dc] = p.dma("gpsimd", t_dwb[par][:], dw[h, dc], sDW[par],
                                      waits=xw)

                dw_slab(0)
                for dc in range(DC + 2):
                    if dc < DC:
                        par = dc % 2
                        if dc + 1 < DC:
                            dw_slab(dc + 1)
                        if dc >= 2:
                            down_epilogue(dc - 2)
                        w = [(sDW[par], dtoks[dc]), (V, pd_freeV[par])]
                        t_dn = None
                        for c in range(IC_H):
                            for th in range(TH):
                                last = (c == IC_H - 1 and th == TH - 1)
                                t_dn = p.emit("tensor", lambda e, par=par, th=th, c=c: e.matmul(
                                    pd[par][th][:], t_dwb[par][:, c * 128:(c + 1) * 128],
                                    t_mid[:, c * R + th * TM: c * R + (th + 1) * TM],
                                    start=(c == 0), stop=(c == IC_H - 1)), waits=w,
                                    inc=(True if last else None))
                                w = []
                        t_dns[dc] = t_dn
                        dw_freeT[par] = t_dn
                    else:
                        down_epilogue(dc - 2)

    p.wait_only("gpsimd", [(sDO[0], 16 * do_uses[0]), (sDO[1], 16 * do_uses[1]),
                           (sAG, p.cnt[sAG]), (sO0, o0_done)])
    p.finalize()
    return nc, stack


def make_consts():
    k_all = np.arange(KC * 128)
    l_all = np.arange(L)
    kl = (np.outer(l_all, k_all) % L).astype(np.float64) * (2 * np.pi / L)
    cosm = np.cos(kl)
    sinm = -np.sin(kl)
    cosm[:, 2049:] = 0.0
    sinm[:, 2049:] = 0.0

    def to_slabs(mat):
        m4 = mat.reshape(32, 128, KC, 128)
        m5 = m4.transpose(2, 0, 1, 3).reshape(KC, 4, 8, 128, 128)
        return m5.transpose(0, 1, 3, 2, 4)
    cs = to_slabs(cosm)
    sn = to_slabs(sinm)
    csq = np.stack([cs, sn], axis=3)
    csq = np.ascontiguousarray(csq.reshape(KC, 4, 128, 2048)).astype(BF16NP)

    wk = np.full(KC * 128, 2.0)
    wk[0] = 1.0
    wk[2048] = 1.0
    wk[2049:] = 0.0
    tau = np.arange(TAU)
    kt = (np.outer(k_all, tau) % L).astype(np.float64) * (2 * np.pi / L)
    dm = np.cos(kt) * (wk / L)[:, None]
    dm[:, 2049:] = 0.0
    dm = dm.reshape(KC, 128, TAU).astype(BF16NP)
    return csq, np.ascontiguousarray(dm)


_CONSTS = None


def prep_inputs(L0_attn_x, L0_ffn_gate_w, L0_ffn_gate_s, L0_ffn_up_w, L0_ffn_up_s,
                L0_ffn_down_w, L0_ffn_down_s, R):
    global _CONSTS
    if _CONSTS is None:
        _CONSTS = make_consts()
    csq, dm = _CONSTS
    x = np.ascontiguousarray(np.asarray(L0_attn_x, np.float32).reshape(-1, L))
    gwt = np.ascontiguousarray(np.asarray(L0_ffn_gate_w, np.int32)
                               .reshape(2, IC_H, 128, DC, 128).transpose(0, 1, 4, 3, 2))
    uwt = np.ascontiguousarray(np.asarray(L0_ffn_up_w, np.int32)
                               .reshape(2, IC_H, 128, DC, 128).transpose(0, 1, 4, 3, 2))
    dwt = np.ascontiguousarray(np.asarray(L0_ffn_down_w, np.int32)
                               .reshape(DC, 128, 2, IC_H, 128).transpose(2, 0, 4, 3, 1))
    scl = np.zeros((128, 4), np.float32)
    scl[:, 0] = np.float32(np.asarray(L0_ffn_gate_s).reshape(-1)[0])
    scl[:, 1] = np.float32(np.asarray(L0_ffn_up_s).reshape(-1)[0])
    scl[:, 2] = np.float32(np.asarray(L0_ffn_down_s).reshape(-1)[0])
    common = {
        "gw": gwt, "uw": uwt, "dw": dwt, "csq": csq, "dmat": dm, "scl": scl,
        "idbf": np.eye(128, dtype=BF16NP), "idf": np.eye(128, dtype=np.float32),
    }
    ncores = x.shape[0] // R
    in_maps = [dict(common, x=np.ascontiguousarray(x[c * R:(c + 1) * R]))
               for c in range(ncores)]
    return in_maps


def kernel(L0_attn_x, L0_ffn_gate_w, L0_ffn_gate_s, L0_ffn_up_w, L0_ffn_up_s,
           L0_ffn_down_w, L0_ffn_down_s):
    R = 1024
    in_maps = prep_inputs(L0_attn_x, L0_ffn_gate_w, L0_ffn_gate_s, L0_ffn_up_w,
                          L0_ffn_up_s, L0_ffn_down_w, L0_ffn_down_s, R)
    nc, stack = build_program(R)
    try:
        res = run_bass_kernel_spmd(nc, in_maps, core_ids=list(range(NCORES)))
    finally:
        stack.close()
    out = np.concatenate([np.asarray(r["out"], np.float32) for r in res.results], axis=0)
    return out.reshape(2, L, L)


# version 14
# speedup vs baseline: 1.0966x; 1.0966x over previous
import contextlib

import numpy as np
import ml_dtypes

import concourse.bass as bass
import concourse.mybir as mybir
from concourse.bass_utils import run_bass_kernel_spmd

F32 = mybir.dt.float32
BF16 = mybir.dt.bfloat16
I32 = mybir.dt.int32
U32 = mybir.dt.uint32
AF = mybir.ActivationFunctionType
ALU = mybir.AluOpType
AX = mybir.AxisListType
BF16NP = ml_dtypes.bfloat16

L = 4096
NCORES = 8
KC = 17
TAU = 2560
I_DIM = 11008
IC_H = 43
DC = 32
EPS = 1e-5


class Prog:
    ENGINES = ("tensor", "vector", "scalar", "gpsimd", "sync")

    def __init__(self, nc, stack):
        self.nc = nc
        self.stack = stack
        self.streams = {e: [] for e in self.ENGINES}
        self.cnt = {}
        self.esem = {e: None for e in self.ENGINES}
        for e in self.ENGINES:
            self.esem[e] = self.sem(f"ms_{e}")

    def sem(self, name):
        h = self.stack.enter_context(self.nc.semaphore(name))
        self.cnt[h] = 0
        return h

    def emit(self, engine, fn, waits=(), inc=None, inc_by=1):
        waits = tuple((s, v) for (s, v) in waits if v > 0)
        for s, v in waits:
            assert v <= self.cnt[s], f"wait beyond emitted on {s.name}: {v} > {self.cnt[s]}"
        if inc is True:
            inc = self.esem[engine]

        def op(eng, fn=fn, waits=waits, inc=inc, inc_by=inc_by):
            for (s, v) in waits:
                eng.wait_ge(s, v)
            inst = fn(eng)
            if inc is not None:
                inst.then_inc(inc, inc_by)
        self.streams[engine].append(op)
        if inc is not None:
            self.cnt[inc] += inc_by
            return self.cnt[inc]
        return None

    def milestone(self, engine, waits=()):
        return self.emit(engine, lambda e: e.drain(), waits=waits, inc=True)

    def dma(self, q, out, in_, sem, waits=(), **kw):
        return self.emit(q, lambda eng: eng.dma_start(out=out, in_=in_, **kw),
                         waits=waits, inc=sem, inc_by=16)

    def wait_only(self, engine, waits):
        waits = tuple((s, v) for (s, v) in waits if v > 0)
        for s, v in waits:
            assert v <= self.cnt[s]

        def op(eng, waits=waits):
            for (s, v) in waits:
                eng.wait_ge(s, v)
        self.streams[engine].append(op)

    def join_all(self, engines=None):
        snap = [(s, v) for s, v in self.cnt.items() if v > 0]
        for e in (engines or self.ENGINES):
            self.wait_only(e, snap)

    def finalize(self):
        nc = self.nc
        with nc.Block() as block:
            for e in self.ENGINES:
                fns = self.streams[e]
                if not fns:
                    continue
                def body(eng, fns=fns):
                    for f in fns:
                        f(eng)
                getattr(block, e)(body)


def build_program(R):
    assert R % 512 == 0
    NT = R // 128
    TPS = 4
    NSUB = NT // TPS
    TS = TPS * 128
    TM = 512
    TH = R // TM

    nc = bass.Bass("TRN2", target_bir_lowering=False, debug=False)
    stack = contextlib.ExitStack()
    p = Prog(nc, stack)

    x_in = nc.dram_tensor("x", [R, L], F32, kind="ExternalInput").ap()
    gw = nc.dram_tensor("gw", [2, IC_H, 128, DC, 128], I32, kind="ExternalInput").ap()
    uw = nc.dram_tensor("uw", [2, IC_H, 128, DC, 128], I32, kind="ExternalInput").ap()
    dw = nc.dram_tensor("dw", [2, DC, 128, IC_H, 128], I32, kind="ExternalInput").ap()
    csq = nc.dram_tensor("csq", [KC, 4, 128, 2048], BF16, kind="ExternalInput").ap()
    dmat = nc.dram_tensor("dmat", [KC, 128, TAU], BF16, kind="ExternalInput").ap()
    scl = nc.dram_tensor("scl", [128, 4], F32, kind="ExternalInput").ap()
    idbf = nc.dram_tensor("idbf", [128, 128], BF16, kind="ExternalInput").ap()
    idf = nc.dram_tensor("idf", [128, 128], F32, kind="ExternalInput").ap()
    out = nc.dram_tensor("out", [R, L], F32, kind="ExternalOutput").ap()
    xnd = nc.dram_tensor("xnd", [2 * R, L], BF16, kind="Internal").ap()
    xnd_w = xnd.rearrange("(r two) c -> r (two c)", two=2)

    sb = lambda n, s, d: stack.enter_context(nc.sbuf_tensor(n, s, d))

    t_xnT = sb("xnT", [128, DC * R], BF16)
    t_scl = sb("t_scl", [128, 4], F32)
    t_idb = sb("t_idb", [128, 128], BF16)
    t_idf = sb("t_idf", [128, 128], F32)
    t_eps = sb("t_eps", [128, 1], F32)
    gs_ap = t_scl[:, 0:1]
    us_ap = t_scl[:, 1:2]
    ds_ap = t_scl[:, 2:3]

    sSC = p.sem("sSC")
    sX = [p.sem("sX0"), p.sem("sX1")]
    sXND = [p.sem("sXND0"), p.sem("sXND1")]
    sO0 = p.sem("sO0")
    sAG = p.sem("sAG")
    sCQ = [p.sem(f"sCQ{i}") for i in range(4)]
    sDQ = [p.sem(f"sDQ{i}") for i in range(4)]
    sGA = [p.sem("sGA0"), p.sem("sGA1")]
    sGW = [p.sem("sGW0"), p.sem("sGW1")]
    sUW = [p.sem("sUW0"), p.sem("sUW1")]
    sDW = [p.sem("sDW0"), p.sem("sDW1")]
    sDO = [p.sem("sDO0"), p.sem("sDO1")]

    T, V, A, G = (p.esem["tensor"], p.esem["vector"], p.esem["scalar"], p.esem["gpsimd"])

    p.dma("sync", t_scl[:], scl[:], sSC)
    p.dma("sync", t_idb[:], idbf[:], sSC)
    p.dma("sync", t_idf[:], idf[:], sSC)
    c_pro = p.cnt[sSC]
    o0_done = p.dma("gpsimd", out[:, :], x_in[:, :], sO0)
    v_eps = p.emit("vector", lambda e: e.memset(t_eps[:], EPS), inc=True)

    with contextlib.ExitStack() as attn_sb:
        asb = lambda n, s, d: attn_sb.enter_context(nc.sbuf_tensor(n, s, d))
        t_x = [asb("t_x0", [128, L], F32), asb("t_x1", [128, L], F32)]
        t_xn = asb("t_xn", [128, L], BF16)
        t_stats = asb("t_stats", [128, 48], F32)
        t_mv = asb("t_mv", [128, 2], F32)
        t_sd = asb("t_sd", [128, 1], F32)
        t_rstd = asb("t_rstd", [128, 1], F32)
        t_cq = [asb(f"t_cq{i}", [128, 2048], BF16) for i in range(4)]
        t_dq = [asb(f"t_dq{i}", [128, 1024], BF16) for i in range(4)]
        t_pa = asb("t_pa", [128, TM], F32)
        t_pb = asb("t_pb", [128, TM], F32)
        t_P = asb("t_P", [128, KC * TM], BF16)
        t_corr = [asb(f"t_corr{i}", [128, TAU], BF16) for i in range(TPS)]
        t_max8 = asb("t_max8", [128, 8], F32)
        t_sub8 = asb("t_sub8", [128, 8], F32)
        t_exp8 = asb("t_exp8", [128, 8], F32)
        t_ssum = asb("t_ssum", [128, 1], F32)
        t_rs = asb("t_rs", [128, 1], F32)
        t_w8A = asb("t_w8A", [128, 8 * TPS], F32)
        t_idx8 = asb("t_idx8", [128, 8], U32)
        t_b8A = asb("t_b8A", [128, 8 * NT], U32)
        t_gidxA = asb("t_gidxA", [128, 8 * TPS], U32)
        t_gath = [asb("t_g0", [128, L], BF16), asb("t_g1", [128, L], BF16)]
        t_agg = asb("t_agg", [128, L], BF16)

        st = {
            "x_freeV": [0, 0], "mv_freeA": 0, "v_mv": 0, "tt_last": 0,
            "cq_freeT": [0, 0, 0, 0], "dq_freeT": [0, 0, 0, 0], "ga_freeV": [0, 0],
            "agg_freeG": 0, "xnd_done": {}, "corr_evacV": {}, "tp_freeA_last": 0,
            "fence_ln": None, "fence_corr": None, "v_gidx": 0,
        }
        for gt_ in range(NT):
            g_iota_all = p.emit("gpsimd", lambda e, gt_=gt_: e.iota(
                t_b8A[:, gt_ * 8:(gt_ + 1) * 8], pattern=[[0, 8]],
                base=gt_ * 128 * 2 * L, channel_multiplier=2 * L), inc=True)

        def emit_ln_dft(s_ix, ps):
            aps = lambda n, s, d: ps.enter_context(nc.psum_tensor(n, s, d))
            tp = [aps(f"tp0_{s_ix}", [128, 1024], BF16), aps(f"tp1_{s_ix}", [128, 1024], BF16)]
            fre = [aps(f"fre0_{s_ix}", [128, TM], F32), aps(f"fre1_{s_ix}", [128, TM], F32)]
            fim = [aps(f"fim0_{s_ix}", [128, TM], F32), aps(f"fim1_{s_ix}", [128, TM], F32)]
            fence = st["fence_ln"] or [(V, p.cnt[V]), (A, p.cnt[A])]
            p.wait_only("tensor", fence)

            tp_freeA = [0, 0]
            ln_v = 0
            for g in range(TPS):
                gt = s_ix * TPS + g
                xs = gt % 2
                c_x = p.dma("sync", t_x[xs][:], x_in[gt * 128:(gt + 1) * 128, :], sX[xs],
                            waits=[(V, st["x_freeV"][xs])])
                for kk in range(8):
                    p.emit("vector", lambda e, kk=kk, xs=xs: e.bn_stats(
                        t_stats[:, 6 * kk:6 * kk + 6], t_x[xs][:, 512 * kk:512 * (kk + 1)]),
                        waits=([(sX[xs], c_x), (V, st["v_mv"])] if kk == 0 else []))
                v_s = p.milestone("vector")
                st["v_mv"] = p.emit("vector", lambda e: e.bn_aggr(t_mv[:], t_stats[:]),
                                    waits=[(V, v_s), (A, st["mv_freeA"])], inc=True)
                st["mv_freeA"] = p.emit("scalar", lambda e: e.activation(
                    t_sd[:], t_mv[:, 1:2], AF.Sqrt, bias=t_eps[:]),
                    waits=[(V, st["v_mv"]), (V, v_eps)], inc=True)
                v_rec = p.emit("vector", lambda e: e.reciprocal(t_rstd[:], t_sd[:]),
                               waits=[(A, st["mv_freeA"])], inc=True)
                ln_v = p.emit("vector", lambda e, xs=xs: e.tensor_scalar(
                    t_xn[:], t_x[xs][:], t_mv[:, 0:1], t_rstd[:, 0:1],
                    ALU.subtract, ALU.mult),
                    waits=[(V, v_rec), (T, st["tt_last"]),
                           (sXND[0], p.cnt[sXND[0]]), (sXND[1], p.cnt[sXND[1]])],
                    inc=True)
                st["x_freeV"][xs] = ln_v
                p.dma("scalar", xnd_w[gt * 128:(gt + 1) * 128, 0:L], t_xn[:], sXND[s_ix % 2],
                      waits=[(V, ln_v)])
                p.dma("scalar", xnd_w[gt * 128:(gt + 1) * 128, L:2 * L], t_xn[:], sXND[s_ix % 2])
                for lc in range(DC):
                    tw = [(V, ln_v), (A, tp_freeA[lc % 2])]
                    if gt == 0 and lc == 0:
                        tw.append((sSC, c_pro))
                    st["tt_last"] = p.emit("tensor", lambda e, lc=lc: e.transpose(
                        tp[lc % 2][:, 0:128], t_xn[:, lc * 128:(lc + 1) * 128],
                        t_idb[:]), waits=tw, inc=True)
                    tp_freeA[lc % 2] = p.emit("scalar", lambda e, lc=lc, gt=gt: e.activation(
                        t_xnT[:, lc * R + gt * 128: lc * R + (gt + 1) * 128],
                        tp[lc % 2][:, 0:128], AF.Copy),
                        waits=[(T, st["tt_last"])], inc=True)
            st["xnd_done"][s_ix] = (sXND[s_ix % 2], p.cnt[sXND[s_ix % 2]])
            st["tp_freeA_last"] = max(tp_freeA)

            fe_freeA = [0, 0]
            pa_freeG = 0
            qmile = st["cq_freeT"]
            qtoks = {}
            for q in range(4):
                qtoks[(0, q)] = p.dma("scalar", t_cq[q][:], csq[0, q], sCQ[q],
                                      waits=[(T, qmile[q])])
            for kc in range(KC):
                for lc in range(DC):
                    q, m = lc // 8, lc % 8
                    w = []
                    if m == 0:
                        w.append((sCQ[q], qtoks[(kc, q)]))
                        if lc == 0:
                            w.append((A, st["tp_freeA_last"]))
                            w.append((A, fe_freeA[kc % 2]))
                    rhs = t_xnT[:, bass.ts(lc, R)][:, s_ix * TS: s_ix * TS + TM]
                    p.emit("tensor", lambda e, kc=kc, q=q, m=m, rhs=rhs, lc=lc: e.matmul(
                        fre[kc % 2][:], t_cq[q][:, m * 128:(m + 1) * 128], rhs,
                        start=(lc == 0), stop=(lc == DC - 1)), waits=w)
                    p.emit("tensor", lambda e, kc=kc, q=q, m=m, rhs=rhs, lc=lc: e.matmul(
                        fim[kc % 2][:], t_cq[q][:, 1024 + m * 128:1024 + (m + 1) * 128],
                        rhs, start=(lc == 0), stop=(lc == DC - 1)))
                    if m == 7:
                        qmile[q] = p.milestone("tensor")
                t_kc = qmile[3]
                if kc + 1 < KC:
                    for q in range(4):
                        qtoks[(kc + 1, q)] = p.dma("scalar", t_cq[q][:], csq[kc + 1, q],
                                                   sCQ[q], waits=[(T, qmile[q])])
                p.emit("scalar", lambda e, kc=kc: e.activation(
                    t_pa[:], fre[kc % 2][:], AF.Square),
                    waits=[(T, t_kc), (G, pa_freeG)])
                fe_freeA[kc % 2] = p.emit("scalar", lambda e, kc=kc: e.activation(
                    t_pb[:], fim[kc % 2][:], AF.Square), inc=True)
                pa_freeG = p.emit("gpsimd", lambda e, kc=kc: e.tensor_add(
                    t_P[:, kc * TM:(kc + 1) * TM], t_pa[:], t_pb[:]),
                    waits=[(A, fe_freeA[kc % 2])], inc=True)
            st["v_P"] = pa_freeG
            st["fence_corr"] = [(V, p.cnt[V]), (A, p.cnt[A])]

        def emit_corr(s_ix):
            with contextlib.ExitStack() as ps_b:
                cor = [ps_b.enter_context(nc.psum_tensor(f"cor{i}_{s_ix}", [128, 512], F32))
                       for i in range(8)]
                p.wait_only("tensor", st["fence_corr"])
                sched = [(tq, kc) for tq in range(3) for kc in range(KC)]
                dmile = st["dq_freeT"]
                dtoks = {}

                def emit_d(i):
                    tq, kc = sched[i]
                    slot = i % 4
                    nblk = 2 if tq < 2 else 1
                    dtoks[i] = p.dma("scalar", t_dq[slot][:, 0:512 * nblk],
                                     dmat[kc, :, tq * 1024:tq * 1024 + 512 * nblk],
                                     sDQ[slot], waits=[(T, dmile[slot])])

                for i in range(3):
                    emit_d(i)
                corr_evacA = 0
                for i, (tq, kc) in enumerate(sched):
                    slot = i % 4
                    nblk = 2 if tq < 2 else 1
                    if i + 3 < len(sched):
                        emit_d(i + 3)
                    w = [(sDQ[slot], dtoks[i])]
                    if kc == 0:
                        w.append((G, st["v_P"]))
                        if corr_evacA > 0:
                            w.append((A, corr_evacA))
                    for t in range(TPS):
                        for blk in range(nblk):
                            p.emit("tensor", lambda e, kc=kc, t=t, blk=blk, slot=slot: e.matmul(
                                cor[2 * t + blk][:],
                                t_P[:, kc * TM + t * 128: kc * TM + (t + 1) * 128],
                                t_dq[slot][:, blk * 512:(blk + 1) * 512],
                                start=(kc == 0), stop=(kc == KC - 1)), waits=w)
                            w = []
                    dmile[slot] = p.milestone("tensor")
                    if kc == KC - 1:
                        t_cordone = dmile[slot]
                        for t in range(TPS):
                            for blk in range(nblk):
                                ew = [(T, t_cordone)]
                                if tq == 0 and t == 0 and blk == 0:
                                    ew.append((V, st.get("tail_corr_readV", 0)))
                                corr_evacA = p.emit("scalar", lambda e, t=t, blk=blk, tq=tq: e.activation(
                                    t_corr[t][:, tq * 1024 + blk * 512: tq * 1024 + (blk + 1) * 512],
                                    cor[2 * t + blk][:], AF.Copy), waits=ew, inc=True)
                st["corr_evacV"][s_ix] = corr_evacA
                st["fence_ln"] = [(V, p.cnt[V]), (A, p.cnt[A])]

        def emit_tail_p1(s_ix):
            corr_evacV = st["corr_evacV"][s_ix]
            for g in range(TPS):
                gt = s_ix * TPS + g
                vm = p.emit("vector", lambda e, g=g: e.max(t_max8[:], t_corr[g][:]),
                            waits=[(A, corr_evacV), (V, p.cnt[V])], inc=True)
                p.wait_only("vector", [(V, vm)])
                vi = p.emit("vector", lambda e, g=g: e.max_index(
                    t_idx8[:], t_max8[:], t_corr[g][:]), inc=True)
                p.wait_only("vector", [(V, vi)])
                v_sub = p.emit("vector", lambda e: e.tensor_scalar(
                    t_sub8[:], t_max8[:], t_max8[:, 0:1], None, ALU.subtract),
                    inc=True)
                a_exp = p.emit("scalar", lambda e: e.activation(
                    t_exp8[:], t_sub8[:], AF.Exp), waits=[(V, v_sub), (A, p.cnt[A])],
                    inc=True)
                v_red = p.emit("vector", lambda e: e.reduce_sum(t_ssum[:], t_exp8[:], AX.X),
                               waits=[(A, a_exp)], inc=True)
                v_rs = p.emit("vector", lambda e: e.reciprocal(t_rs[:], t_ssum[:]),
                              waits=[(V, v_red)], inc=True)
                p.emit("vector", lambda e, g=g: e.tensor_scalar(
                    t_w8A[:, g * 8:(g + 1) * 8], t_exp8[:], t_rs[:, 0:1], None, ALU.mult),
                    waits=[(V, v_rs)], inc=True)
                st["v_gidx"] = p.emit("vector", lambda e, g=g, gt=gt: e.tensor_tensor(
                    t_gidxA[:, g * 8:(g + 1) * 8], t_b8A[:, gt * 8:(gt + 1) * 8],
                    t_idx8[:], ALU.add),
                    waits=[(G, g_iota_all)], inc=True)
                st["tail_corr_readV"] = vi
            st["v_w8all"] = st["v_gidx"]

        def emit_tail_p2(s_ix):
            v_w8all = st["v_w8all"]
            xnd_done = st["xnd_done"][s_ix]
            for g in range(TPS):
                gt = s_ix * TPS + g
                v_agg = 0
                for k in range(8):
                    slot = k % 2
                    gtok = p.emit("gpsimd", lambda e, g=g, k=k, slot=slot: e.indirect_dma_start(
                        t_gath[slot][:], None, xnd[:, :],
                        bass.IndirectOffsetOnAxis(ap=t_gidxA[:, g * 8 + k:g * 8 + k + 1],
                                                  axis=1)),
                        waits=[(V, v_w8all), xnd_done, (V, st["ga_freeV"][slot])],
                        inc=sGA[slot], inc_by=16)
                    if k == 0:
                        st["ga_freeV"][slot] = p.emit(
                            "vector", lambda e, g=g, slot=slot: e.tensor_scalar(
                                t_agg[:], t_gath[slot][:], t_w8A[:, g * 8:g * 8 + 1],
                                None, ALU.mult),
                            waits=[(sGA[slot], gtok), (sAG, p.cnt[sAG])], inc=True)
                    else:
                        st["ga_freeV"][slot] = p.emit(
                            "vector", lambda e, g=g, k=k, slot=slot: e.scalar_tensor_tensor(
                                t_agg[:], t_gath[slot][:], t_w8A[:, g * 8 + k:g * 8 + k + 1],
                                t_agg[:], ALU.mult, ALU.add),
                            waits=[(sGA[slot], gtok), (V, v_agg)], inc=True)
                    v_agg = st["ga_freeV"][slot]
                for half in range(2):
                    p.dma("gpsimd", out[gt * 128:(gt + 1) * 128, half * 2048:(half + 1) * 2048],
                          t_agg[:, half * 2048:(half + 1) * 2048], sAG,
                          waits=[(V, v_agg), (sO0, o0_done)], accum_op=ALU.add)

        scopes = {}
        for s_ix in range(NSUB):
            scopes[s_ix] = contextlib.ExitStack()
            emit_ln_dft(s_ix, scopes[s_ix])
            if s_ix > 0:
                emit_tail_p1(s_ix - 1)
                emit_tail_p2(s_ix - 1)
            scopes[s_ix].close()
            emit_corr(s_ix)
        emit_tail_p1(NSUB - 1)
        ffn_post_cb = lambda: emit_tail_p2(NSUB - 1)

    early_fence = [(V, max(st["x_freeV"])),
                   (sX[0], p.cnt[sX[0]]), (sX[1], p.cnt[sX[1]]), (sSC, c_pro)]
    with contextlib.ExitStack() as ffn_sb:
        fsb = lambda n, s, d: ffn_sb.enter_context(nc.sbuf_tensor(n, s, d))
        t_war = fsb("war", [128, 4 * DC * 128], BF16)
        t_gwb = [t_war[:, 0:DC * 128], t_war[:, DC * 128:2 * DC * 128]]
        t_uwb = [t_war[:, 2 * DC * 128:3 * DC * 128], t_war[:, 3 * DC * 128:4 * DC * 128]]
        t_dwb = [t_war[:, 0:IC_H * 128], t_war[:, IC_H * 128:2 * IC_H * 128]]
        t_silu = [fsb("sil0", [128, TM], F32), fsb("sil1", [128, TM], F32)]
        t_dtmp = [fsb("dt0", [128, TM], F32), fsb("dt1", [128, TM], F32)]
        t_dacc = [fsb(f"da{th}", [128, 2048], BF16) for th in range(TH)]
        t_mid = fsb("mid", [128, IC_H * R], BF16)

        gw_freeT = [0, 0]
        dw_freeT = [0, 0]
        do_uses = [0, 0]
        silu_freeV = [0, 0]
        dacc_lastV = [0, 0]
        t_dn = 0
        out_view4 = out.rearrange("(th tq pp) (dc4 c) -> th pp tq dc4 c",
                                  th=TH, tq=4, pp=128, c=512)

        for h in range(2):
            with contextlib.ExitStack() as ps_c:
                pg = [[ps_c.enter_context(nc.psum_tensor(f"pg{par}{th}_{h}", [128, TM], F32))
                       for th in range(TH)] for par in range(2)]
                pu = [[ps_c.enter_context(nc.psum_tensor(f"pu{par}{th}_{h}", [128, TM], F32))
                       for th in range(TH)] for par in range(2)]
                if h == 0:
                    p.wait_only("tensor", st["fence_ln"])
                else:
                    p.wait_only("tensor", [(V, p.cnt[V]), (A, p.cnt[A])])
                mid_freeV = [0, 0]
                for ic in range(IC_H):
                    par = ic % 2
                    xw = [(T, gw_freeT[par])]
                    if ic < 2 and h == 0:
                        xw = list(early_fence)
                    if ic < 2 and h == 1:
                        xw = [(T, t_dn),
                              (sDW[0], p.cnt[sDW[0]]), (sDW[1], p.cnt[sDW[1]])]
                    gtok = p.dma("gpsimd", t_gwb[par][:], gw[h, ic], sGW[par], waits=xw)
                    utok = p.dma("gpsimd", t_uwb[par][:], uw[h, ic], sUW[par])
                    if h == 0 and ic == 3 and ffn_post_cb is not None:
                        ffn_post_cb()
                        ffn_post_cb = None
                    w = [(sGW[par], gtok), (sUW[par], utok), (V, mid_freeV[par])]
                    for dc2 in range(DC):
                        for th in range(TH):
                            p.emit("tensor", lambda e, par=par, th=th, dc2=dc2: e.matmul(
                                pg[par][th][:], t_gwb[par][:, dc2 * 128:(dc2 + 1) * 128],
                                t_xnT[:, dc2 * R + th * TM: dc2 * R + (th + 1) * TM],
                                start=(dc2 == 0), stop=(dc2 == DC - 1)), waits=w)
                            w = []
                        for th in range(TH):
                            p.emit("tensor", lambda e, par=par, th=th, dc2=dc2: e.matmul(
                                pu[par][th][:], t_uwb[par][:, dc2 * 128:(dc2 + 1) * 128],
                                t_xnT[:, dc2 * R + th * TM: dc2 * R + (th + 1) * TM],
                                start=(dc2 == 0), stop=(dc2 == DC - 1)))
                    t_gu = p.milestone("tensor")
                    gw_freeT[par] = t_gu
                    for th in range(TH):
                        sw = [(T, t_gu), (sSC, c_pro), (V, silu_freeV[th])]
                        if h == 0 and ic < 2:
                            sw += [(sXND[0], p.cnt[sXND[0]]), (sXND[1], p.cnt[sXND[1]])]
                        a_si = p.emit("scalar", lambda e, par=par, th=th: e.activation(
                            t_silu[th][:], pg[par][th][:], AF.Silu, scale=gs_ap),
                            waits=sw, inc=True)
                        mw = [(A, a_si)]
                        if h == 0 and ic == 14 and th == 0:
                            mw = mw + [(V, p.cnt[V]), (G, p.cnt[G]),
                                       (sAG, p.cnt[sAG]),
                                       (sGA[0], p.cnt[sGA[0]]),
                                       (sGA[1], p.cnt[sGA[1]])]
                        mid_freeV[par] = silu_freeV[th] = p.emit(
                            "vector",
                            lambda e, par=par, th=th, ic=ic: e.scalar_tensor_tensor(
                                t_mid[:, ic * R + th * TM: ic * R + (th + 1) * TM],
                                pu[par][th][:], us_ap, t_silu[th][:], ALU.mult, ALU.mult),
                            waits=mw, inc=True)

            with contextlib.ExitStack() as ps_d:
                pd = [[ps_d.enter_context(nc.psum_tensor(f"pd{par}{th}_{h}", [128, TM], F32))
                       for th in range(TH)] for par in range(2)]
                pt = [ps_d.enter_context(nc.psum_tensor(f"pt{par}_{h}", [128, 512], F32))
                      for par in range(2)]
                p.wait_only("tensor", [(V, p.cnt[V]), (A, p.cnt[A])])
                ag_total = p.cnt[sAG]
                dtmp_freeT = [0, 0]
                pd_freeV = [0, 0]
                pt_freeV = [0, 0]
                t_dns = {}

                def down_epilogue(dc):
                    par = dc % 2
                    dcm = dc % 4
                    for th in range(TH):
                        tpar = (dc * TH + th) % 2
                        pd_freeV[par] = p.emit(
                            "vector", lambda e, par=par, th=th, tpar=tpar: e.tensor_scalar(
                                t_dtmp[tpar][:], pd[par][th][:], ds_ap, None, ALU.mult),
                            waits=[(T, t_dns[dc]), (T, dtmp_freeT[tpar])], inc=True)
                        w = [(V, pd_freeV[par]), (V, pt_freeV[tpar])]
                        for tq in range(4):
                            p.emit("tensor", lambda e, tq=tq, tpar=tpar: e.transpose(
                                pt[tpar][:, tq * 128:(tq + 1) * 128],
                                t_dtmp[tpar][:, tq * 128:(tq + 1) * 128], t_idf[:]),
                                waits=w)
                            w = []
                        dtmp_freeT[tpar] = p.milestone("tensor")
                        ew = [(T, dtmp_freeT[tpar]), (V, pt_freeV[tpar])]
                        if dcm == 0:
                            ew.append((sDO[th], 16 * do_uses[th]))
                        pt_freeV[tpar] = dacc_lastV[th] = p.emit(
                            "vector", lambda e, th=th, tpar=tpar, dcm=dcm: e.tensor_copy(
                                t_dacc[th].rearrange("p (tq dd j) -> p tq dd j",
                                                     tq=4, dd=4)[:, :, dcm, :],
                                pt[tpar].rearrange("p (tq j) -> p tq j", tq=4)),
                            waits=ew, inc=True)
                        if dcm == 3:
                            do_uses[th] += 1
                            p.emit("gpsimd", lambda e, th=th, dc=dc: e.dma_start(
                                out=out_view4[th, :, :, dc // 4, :],
                                in_=t_dacc[th].rearrange("p (tq c) -> p tq c", tq=4),
                                accum_op=ALU.add),
                                waits=[(V, dacc_lastV[th]), (sO0, o0_done),
                                       (sAG, ag_total)],
                                inc=sDO[th], inc_by=16)

                dtoks = {}

                def dw_slab(dc):
                    par = dc % 2
                    xw = [(T, dw_freeT[par])]
                    if dc < 2:
                        xw = [(T, gw_freeT[0]), (T, gw_freeT[1]),
                              (sGW[0], p.cnt[sGW[0]]), (sGW[1], p.cnt[sGW[1]]),
                              (sUW[0], p.cnt[sUW[0]]), (sUW[1], p.cnt[sUW[1]]),
                              (V, p.cnt[V]), (A, p.cnt[A])]
                    dtoks[dc] = p.dma("gpsimd", t_dwb[par][:], dw[h, dc], sDW[par],
                                      waits=xw)

                dw_slab(0)
                for dc in range(DC + 2):
                    if dc < DC:
                        par = dc % 2
                        if dc + 1 < DC:
                            dw_slab(dc + 1)
                        if dc >= 2:
                            down_epilogue(dc - 2)
                        w = [(sDW[par], dtoks[dc]), (V, pd_freeV[par])]
                        for c in range(IC_H):
                            for th in range(TH):
                                p.emit("tensor", lambda e, par=par, th=th, c=c: e.matmul(
                                    pd[par][th][:], t_dwb[par][:, c * 128:(c + 1) * 128],
                                    t_mid[:, c * R + th * TM: c * R + (th + 1) * TM],
                                    start=(c == 0), stop=(c == IC_H - 1)), waits=w)
                                w = []
                        t_dns[dc] = t_dn = p.milestone("tensor")
                        dw_freeT[par] = t_dn
                    else:
                        down_epilogue(dc - 2)

    p.wait_only("gpsimd", [(sDO[0], 16 * do_uses[0]), (sDO[1], 16 * do_uses[1]),
                           (sAG, p.cnt[sAG]), (sO0, o0_done)])
    p.finalize()
    return nc, stack


def make_consts():
    k_all = np.arange(KC * 128)
    l_all = np.arange(L)
    kl = (np.outer(l_all, k_all) % L).astype(np.float64) * (2 * np.pi / L)
    cosm = np.cos(kl)
    sinm = -np.sin(kl)
    cosm[:, 2049:] = 0.0
    sinm[:, 2049:] = 0.0

    def to_slabs(mat):
        m4 = mat.reshape(32, 128, KC, 128)
        m5 = m4.transpose(2, 0, 1, 3).reshape(KC, 4, 8, 128, 128)
        return m5.transpose(0, 1, 3, 2, 4)
    cs = to_slabs(cosm)
    sn = to_slabs(sinm)
    csq = np.stack([cs, sn], axis=3)
    csq = np.ascontiguousarray(csq.reshape(KC, 4, 128, 2048)).astype(BF16NP)

    wk = np.full(KC * 128, 2.0)
    wk[0] = 1.0
    wk[2048] = 1.0
    wk[2049:] = 0.0
    tau = np.arange(TAU)
    kt = (np.outer(k_all, tau) % L).astype(np.float64) * (2 * np.pi / L)
    dm = np.cos(kt) * (wk / L)[:, None]
    dm[:, 2049:] = 0.0
    dm = dm.reshape(KC, 128, TAU).astype(BF16NP)
    return csq, np.ascontiguousarray(dm)


_CONSTS = None


def prep_inputs(L0_attn_x, L0_ffn_gate_w, L0_ffn_gate_s, L0_ffn_up_w, L0_ffn_up_s,
                L0_ffn_down_w, L0_ffn_down_s, R):
    global _CONSTS
    if _CONSTS is None:
        _CONSTS = make_consts()
    csq, dm = _CONSTS
    x = np.ascontiguousarray(np.asarray(L0_attn_x, np.float32).reshape(-1, L))
    gwt = np.ascontiguousarray(np.asarray(L0_ffn_gate_w, np.int32)
                               .reshape(2, IC_H, 128, DC, 128).transpose(0, 1, 4, 3, 2))
    uwt = np.ascontiguousarray(np.asarray(L0_ffn_up_w, np.int32)
                               .reshape(2, IC_H, 128, DC, 128).transpose(0, 1, 4, 3, 2))
    dwt = np.ascontiguousarray(np.asarray(L0_ffn_down_w, np.int32)
                               .reshape(DC, 128, 2, IC_H, 128).transpose(2, 0, 4, 3, 1))
    scl = np.zeros((128, 4), np.float32)
    scl[:, 0] = np.float32(np.asarray(L0_ffn_gate_s).reshape(-1)[0])
    scl[:, 1] = np.float32(np.asarray(L0_ffn_up_s).reshape(-1)[0])
    scl[:, 2] = np.float32(np.asarray(L0_ffn_down_s).reshape(-1)[0])
    common = {
        "gw": gwt, "uw": uwt, "dw": dwt, "csq": csq, "dmat": dm, "scl": scl,
        "idbf": np.eye(128, dtype=BF16NP), "idf": np.eye(128, dtype=np.float32),
    }
    ncores = x.shape[0] // R
    in_maps = [dict(common, x=np.ascontiguousarray(x[c * R:(c + 1) * R]))
               for c in range(ncores)]
    return in_maps


def kernel(L0_attn_x, L0_ffn_gate_w, L0_ffn_gate_s, L0_ffn_up_w, L0_ffn_up_s,
           L0_ffn_down_w, L0_ffn_down_s):
    R = 1024
    in_maps = prep_inputs(L0_attn_x, L0_ffn_gate_w, L0_ffn_gate_s, L0_ffn_up_w,
                          L0_ffn_up_s, L0_ffn_down_w, L0_ffn_down_s, R)
    nc, stack = build_program(R)
    try:
        res = run_bass_kernel_spmd(nc, in_maps, core_ids=list(range(NCORES)))
    finally:
        stack.close()
    out = np.concatenate([np.asarray(r["out"], np.float32) for r in res.results], axis=0)
    return out.reshape(2, L, L)


# version 15
# speedup vs baseline: 1.1193x; 1.0207x over previous
import contextlib

import numpy as np
import ml_dtypes

import concourse.bass as bass
import concourse.mybir as mybir
from concourse.bass_utils import run_bass_kernel_spmd

F32 = mybir.dt.float32
BF16 = mybir.dt.bfloat16
I32 = mybir.dt.int32
U32 = mybir.dt.uint32
AF = mybir.ActivationFunctionType
ALU = mybir.AluOpType
AX = mybir.AxisListType
BF16NP = ml_dtypes.bfloat16

L = 4096
NCORES = 8
KC = 17
TAU = 2560
I_DIM = 11008
IC_H = 43
DC = 32
EPS = 1e-5


class Prog:
    ENGINES = ("tensor", "vector", "scalar", "gpsimd", "sync")

    def __init__(self, nc, stack):
        self.nc = nc
        self.stack = stack
        self.streams = {e: [] for e in self.ENGINES}
        self.cnt = {}
        self.esem = {e: None for e in self.ENGINES}
        for e in self.ENGINES:
            self.esem[e] = self.sem(f"ms_{e}")

    def sem(self, name):
        h = self.stack.enter_context(self.nc.semaphore(name))
        self.cnt[h] = 0
        return h

    def emit(self, engine, fn, waits=(), inc=None, inc_by=1):
        waits = tuple((s, v) for (s, v) in waits if v > 0)
        for s, v in waits:
            assert v <= self.cnt[s], f"wait beyond emitted on {s.name}: {v} > {self.cnt[s]}"
        if inc is True:
            inc = self.esem[engine]

        def op(eng, fn=fn, waits=waits, inc=inc, inc_by=inc_by):
            for (s, v) in waits:
                eng.wait_ge(s, v)
            inst = fn(eng)
            if inc is not None:
                inst.then_inc(inc, inc_by)
        self.streams[engine].append(op)
        if inc is not None:
            self.cnt[inc] += inc_by
            return self.cnt[inc]
        return None

    def milestone(self, engine, waits=()):
        return self.emit(engine, lambda e: e.drain(), waits=waits, inc=True)

    def dma(self, q, out, in_, sem, waits=(), **kw):
        return self.emit(q, lambda eng: eng.dma_start(out=out, in_=in_, **kw),
                         waits=waits, inc=sem, inc_by=16)

    def wait_only(self, engine, waits):
        waits = tuple((s, v) for (s, v) in waits if v > 0)
        for s, v in waits:
            assert v <= self.cnt[s]

        def op(eng, waits=waits):
            for (s, v) in waits:
                eng.wait_ge(s, v)
        self.streams[engine].append(op)

    def join_all(self, engines=None):
        snap = [(s, v) for s, v in self.cnt.items() if v > 0]
        for e in (engines or self.ENGINES):
            self.wait_only(e, snap)

    def finalize(self):
        nc = self.nc
        with nc.Block() as block:
            for e in self.ENGINES:
                fns = self.streams[e]
                if not fns:
                    continue
                def body(eng, fns=fns):
                    for f in fns:
                        f(eng)
                getattr(block, e)(body)


def build_program(R):
    assert R % 512 == 0
    NT = R // 128
    TPS = 4
    NSUB = NT // TPS
    TS = TPS * 128
    TM = 512
    TH = R // TM

    nc = bass.Bass("TRN2", target_bir_lowering=False, debug=False)
    stack = contextlib.ExitStack()
    p = Prog(nc, stack)

    x_in = nc.dram_tensor("x", [R, L], F32, kind="ExternalInput").ap()
    gw = nc.dram_tensor("gw", [2, IC_H, 128, DC, 128], I32, kind="ExternalInput").ap()
    uw = nc.dram_tensor("uw", [2, IC_H, 128, DC, 128], I32, kind="ExternalInput").ap()
    dw = nc.dram_tensor("dw", [2, DC, 128, IC_H, 128], I32, kind="ExternalInput").ap()
    csq = nc.dram_tensor("csq", [KC, 4, 128, 2048], BF16, kind="ExternalInput").ap()
    dmat = nc.dram_tensor("dmat", [KC, 128, TAU], BF16, kind="ExternalInput").ap()
    scl = nc.dram_tensor("scl", [128, 4], F32, kind="ExternalInput").ap()
    idbf = nc.dram_tensor("idbf", [128, 128], BF16, kind="ExternalInput").ap()
    idf = nc.dram_tensor("idf", [128, 128], F32, kind="ExternalInput").ap()
    out = nc.dram_tensor("out", [R, L], F32, kind="ExternalOutput").ap()
    xnd = nc.dram_tensor("xnd", [2 * R, L], BF16, kind="Internal").ap()
    xnd_w = xnd.rearrange("(r two) c -> r (two c)", two=2)

    sb = lambda n, s, d: stack.enter_context(nc.sbuf_tensor(n, s, d))

    t_xnT = sb("xnT", [128, DC * R], BF16)
    t_scl = sb("t_scl", [128, 4], F32)
    t_idb = sb("t_idb", [128, 128], BF16)
    t_idf = sb("t_idf", [128, 128], F32)
    t_eps = sb("t_eps", [128, 1], F32)
    gs_ap = t_scl[:, 0:1]
    us_ap = t_scl[:, 1:2]
    ds_ap = t_scl[:, 2:3]

    sSC = p.sem("sSC")
    sX = [p.sem("sX0"), p.sem("sX1")]
    sXND = [p.sem("sXND0"), p.sem("sXND1")]
    sO0 = p.sem("sO0")
    sAG = p.sem("sAG")
    sCQ = [p.sem(f"sCQ{i}") for i in range(4)]
    sDQ = [p.sem(f"sDQ{i}") for i in range(4)]
    sGA = [p.sem(f"sGA{i}") for i in range(3)]
    sGW = [p.sem("sGW0"), p.sem("sGW1")]
    sUW = [p.sem("sUW0"), p.sem("sUW1")]
    sDW = [p.sem("sDW0"), p.sem("sDW1")]
    sDO = [p.sem("sDO0"), p.sem("sDO1")]

    T, V, A, G = (p.esem["tensor"], p.esem["vector"], p.esem["scalar"], p.esem["gpsimd"])

    p.dma("sync", t_scl[:], scl[:], sSC)
    p.dma("sync", t_idb[:], idbf[:], sSC)
    p.dma("sync", t_idf[:], idf[:], sSC)
    c_pro = p.cnt[sSC]
    o0_done = p.dma("scalar", out[:, :], x_in[:, :], sO0)
    v_eps = p.emit("vector", lambda e: e.memset(t_eps[:], EPS), inc=True)

    with contextlib.ExitStack() as attn_sb:
        asb = lambda n, s, d: attn_sb.enter_context(nc.sbuf_tensor(n, s, d))
        t_x = [asb("t_x0", [128, L], F32), asb("t_x1", [128, L], F32)]
        t_xn = asb("t_xn", [128, L], BF16)
        t_stats = asb("t_stats", [128, 48], F32)
        t_mv = asb("t_mv", [128, 2], F32)
        t_sd = asb("t_sd", [128, 1], F32)
        t_rstd = asb("t_rstd", [128, 1], F32)
        t_cq = [asb(f"t_cq{i}", [128, 2048], BF16) for i in range(4)]
        t_dq = [asb(f"t_dq{i}", [128, 1024], BF16) for i in range(4)]
        t_pa = asb("t_pa", [128, TM], F32)
        t_pb = asb("t_pb", [128, TM], F32)
        t_P = asb("t_P", [128, KC * TM], BF16)
        t_corr = [asb(f"t_corr{i}", [128, TAU], BF16) for i in range(TPS)]
        t_max8 = asb("t_max8", [128, 8], F32)
        t_sub8 = asb("t_sub8", [128, 8], F32)
        t_exp8 = asb("t_exp8", [128, 8], F32)
        t_ssum = asb("t_ssum", [128, 1], F32)
        t_rs = asb("t_rs", [128, 1], F32)
        t_w8A = asb("t_w8A", [128, 8 * TPS], F32)
        t_idx8 = asb("t_idx8", [128, 8], U32)
        t_b8A = asb("t_b8A", [128, 8 * NT], U32)
        t_gidxA = asb("t_gidxA", [128, 8 * TPS], U32)
        t_gath = [asb(f"t_g{i}", [128, L], BF16) for i in range(3)]
        t_agg = asb("t_agg", [128, L], BF16)

        st = {
            "x_freeV": [0, 0], "mv_freeA": 0, "v_mv": 0, "tt_last": 0,
            "cq_freeT": [0, 0, 0, 0], "dq_freeT": [0, 0, 0, 0], "ga_freeV": [0, 0, 0],
            "agg_freeG": 0, "xnd_done": {}, "corr_evacV": {}, "tp_freeA_last": 0,
            "fence_ln": None, "fence_corr": None, "v_gidx": 0,
        }
        for gt_ in range(NT):
            g_iota_all = p.emit("gpsimd", lambda e, gt_=gt_: e.iota(
                t_b8A[:, gt_ * 8:(gt_ + 1) * 8], pattern=[[0, 8]],
                base=gt_ * 128 * 2 * L, channel_multiplier=2 * L), inc=True)

        def emit_ln_dft(s_ix, ps):
            aps = lambda n, s, d: ps.enter_context(nc.psum_tensor(n, s, d))
            tp = [aps(f"tp0_{s_ix}", [128, 1024], BF16), aps(f"tp1_{s_ix}", [128, 1024], BF16)]
            fre = [aps(f"fre0_{s_ix}", [128, TM], F32), aps(f"fre1_{s_ix}", [128, TM], F32)]
            fim = [aps(f"fim0_{s_ix}", [128, TM], F32), aps(f"fim1_{s_ix}", [128, TM], F32)]
            fence = st["fence_ln"] or [(V, p.cnt[V]), (A, p.cnt[A])]
            p.wait_only("tensor", fence)

            tp_freeA = [0, 0]
            ln_v = 0
            for g in range(TPS):
                gt = s_ix * TPS + g
                xs = gt % 2
                c_x = p.dma("sync", t_x[xs][:], x_in[gt * 128:(gt + 1) * 128, :], sX[xs],
                            waits=[(V, st["x_freeV"][xs])])
                for kk in range(8):
                    p.emit("vector", lambda e, kk=kk, xs=xs: e.bn_stats(
                        t_stats[:, 6 * kk:6 * kk + 6], t_x[xs][:, 512 * kk:512 * (kk + 1)]),
                        waits=([(sX[xs], c_x), (V, st["v_mv"])] if kk == 0 else []))
                v_s = p.milestone("vector")
                st["v_mv"] = p.emit("vector", lambda e: e.bn_aggr(t_mv[:], t_stats[:]),
                                    waits=[(V, v_s), (A, st["mv_freeA"])], inc=True)
                st["mv_freeA"] = p.emit("scalar", lambda e: e.activation(
                    t_sd[:], t_mv[:, 1:2], AF.Sqrt, bias=t_eps[:]),
                    waits=[(V, st["v_mv"]), (V, v_eps)], inc=True)
                v_rec = p.emit("vector", lambda e: e.reciprocal(t_rstd[:], t_sd[:]),
                               waits=[(A, st["mv_freeA"])], inc=True)
                ln_v = p.emit("vector", lambda e, xs=xs: e.tensor_scalar(
                    t_xn[:], t_x[xs][:], t_mv[:, 0:1], t_rstd[:, 0:1],
                    ALU.subtract, ALU.mult),
                    waits=[(V, v_rec), (T, st["tt_last"]),
                           (sXND[0], p.cnt[sXND[0]]), (sXND[1], p.cnt[sXND[1]])],
                    inc=True)
                st["x_freeV"][xs] = ln_v
                p.dma("scalar", xnd_w[gt * 128:(gt + 1) * 128, 0:L], t_xn[:], sXND[s_ix % 2],
                      waits=[(V, ln_v)])
                p.dma("scalar", xnd_w[gt * 128:(gt + 1) * 128, L:2 * L], t_xn[:], sXND[s_ix % 2])
                for lc in range(DC):
                    tw = [(V, ln_v), (A, tp_freeA[lc % 2])]
                    if gt == 0 and lc == 0:
                        tw.append((sSC, c_pro))
                    st["tt_last"] = p.emit("tensor", lambda e, lc=lc: e.transpose(
                        tp[lc % 2][:, 0:128], t_xn[:, lc * 128:(lc + 1) * 128],
                        t_idb[:]), waits=tw, inc=True)
                    tp_freeA[lc % 2] = p.emit("scalar", lambda e, lc=lc, gt=gt: e.activation(
                        t_xnT[:, lc * R + gt * 128: lc * R + (gt + 1) * 128],
                        tp[lc % 2][:, 0:128], AF.Copy),
                        waits=[(T, st["tt_last"])], inc=True)
            st["xnd_done"][s_ix] = (sXND[s_ix % 2], p.cnt[sXND[s_ix % 2]])
            st["tp_freeA_last"] = max(tp_freeA)

            fe_freeA = [0, 0]
            pa_freeG = 0
            qmile = st["cq_freeT"]
            qtoks = {}
            for q in range(4):
                qtoks[(0, q)] = p.dma("scalar", t_cq[q][:], csq[0, q], sCQ[q],
                                      waits=[(T, qmile[q])])
            for kc in range(KC):
                for lc in range(DC):
                    q, m = lc // 8, lc % 8
                    w = []
                    if m == 0:
                        w.append((sCQ[q], qtoks[(kc, q)]))
                        if lc == 0:
                            w.append((A, st["tp_freeA_last"]))
                            w.append((A, fe_freeA[kc % 2]))
                    rhs = t_xnT[:, bass.ts(lc, R)][:, s_ix * TS: s_ix * TS + TM]
                    p.emit("tensor", lambda e, kc=kc, q=q, m=m, rhs=rhs, lc=lc: e.matmul(
                        fre[kc % 2][:], t_cq[q][:, m * 128:(m + 1) * 128], rhs,
                        start=(lc == 0), stop=(lc == DC - 1)), waits=w)
                    p.emit("tensor", lambda e, kc=kc, q=q, m=m, rhs=rhs, lc=lc: e.matmul(
                        fim[kc % 2][:], t_cq[q][:, 1024 + m * 128:1024 + (m + 1) * 128],
                        rhs, start=(lc == 0), stop=(lc == DC - 1)))
                    if m == 7:
                        qmile[q] = p.milestone("tensor")
                t_kc = qmile[3]
                if kc + 1 < KC:
                    for q in range(4):
                        qtoks[(kc + 1, q)] = p.dma("scalar", t_cq[q][:], csq[kc + 1, q],
                                                   sCQ[q], waits=[(T, qmile[q])])
                p.emit("scalar", lambda e, kc=kc: e.activation(
                    t_pa[:], fre[kc % 2][:], AF.Square),
                    waits=[(T, t_kc), (G, pa_freeG)])
                fe_freeA[kc % 2] = p.emit("scalar", lambda e, kc=kc: e.activation(
                    t_pb[:], fim[kc % 2][:], AF.Square), inc=True)
                pa_freeG = p.emit("gpsimd", lambda e, kc=kc: e.tensor_add(
                    t_P[:, kc * TM:(kc + 1) * TM], t_pa[:], t_pb[:]),
                    waits=[(A, fe_freeA[kc % 2])], inc=True)
            st["v_P"] = pa_freeG
            st["fence_corr"] = [(V, p.cnt[V]), (A, p.cnt[A])]

        def emit_corr(s_ix):
            with contextlib.ExitStack() as ps_b:
                cor = [ps_b.enter_context(nc.psum_tensor(f"cor{i}_{s_ix}", [128, 512], F32))
                       for i in range(8)]
                p.wait_only("tensor", st["fence_corr"])
                sched = [(tq, kc) for tq in range(3) for kc in range(KC)]
                dmile = st["dq_freeT"]
                dtoks = {}

                def emit_d(i):
                    tq, kc = sched[i]
                    slot = i % 4
                    nblk = 2 if tq < 2 else 1
                    dtoks[i] = p.dma("scalar", t_dq[slot][:, 0:512 * nblk],
                                     dmat[kc, :, tq * 1024:tq * 1024 + 512 * nblk],
                                     sDQ[slot], waits=[(T, dmile[slot])])

                for i in range(3):
                    emit_d(i)
                corr_evacA = 0
                for i, (tq, kc) in enumerate(sched):
                    slot = i % 4
                    nblk = 2 if tq < 2 else 1
                    if i + 3 < len(sched):
                        emit_d(i + 3)
                    w = [(sDQ[slot], dtoks[i])]
                    if kc == 0:
                        w.append((G, st["v_P"]))
                        if corr_evacA > 0:
                            w.append((A, corr_evacA))
                    for t in range(TPS):
                        for blk in range(nblk):
                            p.emit("tensor", lambda e, kc=kc, t=t, blk=blk, slot=slot: e.matmul(
                                cor[2 * t + blk][:],
                                t_P[:, kc * TM + t * 128: kc * TM + (t + 1) * 128],
                                t_dq[slot][:, blk * 512:(blk + 1) * 512],
                                start=(kc == 0), stop=(kc == KC - 1)), waits=w)
                            w = []
                    dmile[slot] = p.milestone("tensor")
                    if kc == KC - 1:
                        t_cordone = dmile[slot]
                        for t in range(TPS):
                            for blk in range(nblk):
                                ew = [(T, t_cordone)]
                                if tq == 0 and t == 0 and blk == 0:
                                    ew.append((V, st.get("tail_corr_readV", 0)))
                                corr_evacA = p.emit("scalar", lambda e, t=t, blk=blk, tq=tq: e.activation(
                                    t_corr[t][:, tq * 1024 + blk * 512: tq * 1024 + (blk + 1) * 512],
                                    cor[2 * t + blk][:], AF.Copy), waits=ew, inc=True)
                st["corr_evacV"][s_ix] = corr_evacA
                st["fence_ln"] = [(V, p.cnt[V]), (A, p.cnt[A])]

        def emit_tail_p1(s_ix):
            corr_evacV = st["corr_evacV"][s_ix]
            for g in range(TPS):
                gt = s_ix * TPS + g
                vm = p.emit("vector", lambda e, g=g: e.max(t_max8[:], t_corr[g][:]),
                            waits=[(A, corr_evacV), (V, p.cnt[V])], inc=True)
                p.wait_only("vector", [(V, vm)])
                vi = p.emit("vector", lambda e, g=g: e.max_index(
                    t_idx8[:], t_max8[:], t_corr[g][:]), inc=True)
                p.wait_only("vector", [(V, vi)])
                v_sub = p.emit("vector", lambda e: e.tensor_scalar(
                    t_sub8[:], t_max8[:], t_max8[:, 0:1], None, ALU.subtract),
                    inc=True)
                a_exp = p.emit("scalar", lambda e: e.activation(
                    t_exp8[:], t_sub8[:], AF.Exp), waits=[(V, v_sub), (A, p.cnt[A])],
                    inc=True)
                v_red = p.emit("vector", lambda e: e.reduce_sum(t_ssum[:], t_exp8[:], AX.X),
                               waits=[(A, a_exp)], inc=True)
                v_rs = p.emit("vector", lambda e: e.reciprocal(t_rs[:], t_ssum[:]),
                              waits=[(V, v_red)], inc=True)
                p.emit("vector", lambda e, g=g: e.tensor_scalar(
                    t_w8A[:, g * 8:(g + 1) * 8], t_exp8[:], t_rs[:, 0:1], None, ALU.mult),
                    waits=[(V, v_rs)], inc=True)
                st["v_gidx"] = p.emit("vector", lambda e, g=g, gt=gt: e.tensor_tensor(
                    t_gidxA[:, g * 8:(g + 1) * 8], t_b8A[:, gt * 8:(gt + 1) * 8],
                    t_idx8[:], ALU.add),
                    waits=[(G, g_iota_all)], inc=True)
                st["tail_corr_readV"] = vi
            st["v_w8all"] = st["v_gidx"]

        def emit_tail_p2(s_ix):
            v_w8all = st["v_w8all"]
            xnd_done = st["xnd_done"][s_ix]
            for g in range(TPS):
                gt = s_ix * TPS + g
                v_agg = 0
                for k in range(8):
                    slot = (g * 8 + k) % 3
                    gtok = p.emit("gpsimd", lambda e, g=g, k=k, slot=slot: e.indirect_dma_start(
                        t_gath[slot][:], None, xnd[:, :],
                        bass.IndirectOffsetOnAxis(ap=t_gidxA[:, g * 8 + k:g * 8 + k + 1],
                                                  axis=1)),
                        waits=[(V, v_w8all), xnd_done, (V, st["ga_freeV"][slot])],
                        inc=sGA[slot], inc_by=16)
                    if k == 0:
                        st["ga_freeV"][slot] = p.emit(
                            "vector", lambda e, g=g, slot=slot: e.tensor_scalar(
                                t_agg[:], t_gath[slot][:], t_w8A[:, g * 8:g * 8 + 1],
                                None, ALU.mult),
                            waits=[(sGA[slot], gtok), (sAG, p.cnt[sAG])], inc=True)
                    else:
                        st["ga_freeV"][slot] = p.emit(
                            "vector", lambda e, g=g, k=k, slot=slot: e.scalar_tensor_tensor(
                                t_agg[:], t_gath[slot][:], t_w8A[:, g * 8 + k:g * 8 + k + 1],
                                t_agg[:], ALU.mult, ALU.add),
                            waits=[(sGA[slot], gtok), (V, v_agg)], inc=True)
                    v_agg = st["ga_freeV"][slot]
                for half in range(2):
                    p.dma("gpsimd", out[gt * 128:(gt + 1) * 128, half * 2048:(half + 1) * 2048],
                          t_agg[:, half * 2048:(half + 1) * 2048], sAG,
                          waits=[(V, v_agg), (sO0, o0_done)], accum_op=ALU.add)

        scopes = {}
        for s_ix in range(NSUB):
            scopes[s_ix] = contextlib.ExitStack()
            emit_ln_dft(s_ix, scopes[s_ix])
            if s_ix > 0:
                emit_tail_p1(s_ix - 1)
                emit_tail_p2(s_ix - 1)
            scopes[s_ix].close()
            emit_corr(s_ix)
        emit_tail_p1(NSUB - 1)
        ffn_post_cb = lambda: emit_tail_p2(NSUB - 1)

    early_fence = [(V, max(st["x_freeV"])),
                   (sX[0], p.cnt[sX[0]]), (sX[1], p.cnt[sX[1]]), (sSC, c_pro)]
    with contextlib.ExitStack() as ffn_sb:
        fsb = lambda n, s, d: ffn_sb.enter_context(nc.sbuf_tensor(n, s, d))
        t_war = fsb("war", [128, 4 * DC * 128], BF16)
        t_gwb = [t_war[:, 0:DC * 128], t_war[:, DC * 128:2 * DC * 128]]
        t_uwb = [t_war[:, 2 * DC * 128:3 * DC * 128], t_war[:, 3 * DC * 128:4 * DC * 128]]
        t_dwb = [t_war[:, 0:IC_H * 128], t_war[:, IC_H * 128:2 * IC_H * 128]]
        t_silu = [fsb("sil0", [128, TM], F32), fsb("sil1", [128, TM], F32)]
        t_dtmp = [fsb("dt0", [128, TM], F32), fsb("dt1", [128, TM], F32)]
        t_dacc = [fsb(f"da{th}", [128, 2048], BF16) for th in range(TH)]
        t_mid = fsb("mid", [128, IC_H * R], BF16)

        gw_freeT = [0, 0]
        dw_freeT = [0, 0]
        do_uses = [0, 0]
        silu_freeV = [0, 0]
        dacc_lastV = [0, 0]
        t_dn = 0
        out_view4 = out.rearrange("(th tq pp) (dc4 c) -> th pp tq dc4 c",
                                  th=TH, tq=4, pp=128, c=512)

        for h in range(2):
            with contextlib.ExitStack() as ps_c:
                pg = [[ps_c.enter_context(nc.psum_tensor(f"pg{par}{th}_{h}", [128, TM], F32))
                       for th in range(TH)] for par in range(2)]
                pu = [[ps_c.enter_context(nc.psum_tensor(f"pu{par}{th}_{h}", [128, TM], F32))
                       for th in range(TH)] for par in range(2)]
                if h == 0:
                    p.wait_only("tensor", st["fence_ln"])
                else:
                    p.wait_only("tensor", [(V, p.cnt[V]), (A, p.cnt[A])])
                mid_freeV = [0, 0]
                for ic in range(IC_H):
                    par = ic % 2
                    xw = [(T, gw_freeT[par])]
                    if ic < 2 and h == 0:
                        xw = list(early_fence)
                    if ic < 2 and h == 1:
                        xw = [(T, t_dn),
                              (sDW[0], p.cnt[sDW[0]]), (sDW[1], p.cnt[sDW[1]])]
                    gtok = p.dma("gpsimd", t_gwb[par][:], gw[h, ic], sGW[par], waits=xw)
                    utok = p.dma("gpsimd", t_uwb[par][:], uw[h, ic], sUW[par])
                    if h == 0 and ic == 3 and ffn_post_cb is not None:
                        ffn_post_cb()
                        ffn_post_cb = None
                    w = [(sGW[par], gtok), (sUW[par], utok), (V, mid_freeV[par])]
                    for dc2 in range(DC):
                        for th in range(TH):
                            p.emit("tensor", lambda e, par=par, th=th, dc2=dc2: e.matmul(
                                pg[par][th][:], t_gwb[par][:, dc2 * 128:(dc2 + 1) * 128],
                                t_xnT[:, dc2 * R + th * TM: dc2 * R + (th + 1) * TM],
                                start=(dc2 == 0), stop=(dc2 == DC - 1)), waits=w)
                            w = []
                        for th in range(TH):
                            p.emit("tensor", lambda e, par=par, th=th, dc2=dc2: e.matmul(
                                pu[par][th][:], t_uwb[par][:, dc2 * 128:(dc2 + 1) * 128],
                                t_xnT[:, dc2 * R + th * TM: dc2 * R + (th + 1) * TM],
                                start=(dc2 == 0), stop=(dc2 == DC - 1)))
                    t_gu = p.milestone("tensor")
                    gw_freeT[par] = t_gu
                    for th in range(TH):
                        sw = [(T, t_gu), (sSC, c_pro), (V, silu_freeV[th])]
                        if h == 0 and ic < 2:
                            sw += [(sXND[0], p.cnt[sXND[0]]), (sXND[1], p.cnt[sXND[1]])]
                        a_si = p.emit("scalar", lambda e, par=par, th=th: e.activation(
                            t_silu[th][:], pg[par][th][:], AF.Silu, scale=gs_ap),
                            waits=sw, inc=True)
                        mw = [(A, a_si)]
                        if h == 0 and ic == 14 and th == 0:
                            mw = mw + [(V, p.cnt[V]), (G, p.cnt[G]),
                                       (sAG, p.cnt[sAG]),
                                       (sGA[0], p.cnt[sGA[0]]),
                                       (sGA[1], p.cnt[sGA[1]]),
                                       (sGA[2], p.cnt[sGA[2]])]
                        mid_freeV[par] = silu_freeV[th] = p.emit(
                            "vector",
                            lambda e, par=par, th=th, ic=ic: e.scalar_tensor_tensor(
                                t_mid[:, ic * R + th * TM: ic * R + (th + 1) * TM],
                                pu[par][th][:], us_ap, t_silu[th][:], ALU.mult, ALU.mult),
                            waits=mw, inc=True)

            with contextlib.ExitStack() as ps_d:
                pd = [[ps_d.enter_context(nc.psum_tensor(f"pd{par}{th}_{h}", [128, TM], F32))
                       for th in range(TH)] for par in range(2)]
                pt = [ps_d.enter_context(nc.psum_tensor(f"pt{par}_{h}", [128, 512], F32))
                      for par in range(2)]
                p.wait_only("tensor", [(V, p.cnt[V]), (A, p.cnt[A])])
                ag_total = p.cnt[sAG]
                dtmp_freeT = [0, 0]
                pd_freeV = [0, 0]
                pt_freeV = [0, 0]
                t_dns = {}

                def down_epilogue(dc):
                    par = dc % 2
                    dcm = dc % 4
                    for th in range(TH):
                        tpar = (dc * TH + th) % 2
                        pd_freeV[par] = p.emit(
                            "vector", lambda e, par=par, th=th, tpar=tpar: e.tensor_scalar(
                                t_dtmp[tpar][:], pd[par][th][:], ds_ap, None, ALU.mult),
                            waits=[(T, t_dns[dc]), (T, dtmp_freeT[tpar])], inc=True)
                        w = [(V, pd_freeV[par]), (V, pt_freeV[tpar])]
                        for tq in range(4):
                            p.emit("tensor", lambda e, tq=tq, tpar=tpar: e.transpose(
                                pt[tpar][:, tq * 128:(tq + 1) * 128],
                                t_dtmp[tpar][:, tq * 128:(tq + 1) * 128], t_idf[:]),
                                waits=w)
                            w = []
                        dtmp_freeT[tpar] = p.milestone("tensor")
                        ew = [(T, dtmp_freeT[tpar]), (V, pt_freeV[tpar])]
                        if dcm == 0:
                            ew.append((sDO[th], 16 * do_uses[th]))
                        pt_freeV[tpar] = dacc_lastV[th] = p.emit(
                            "vector", lambda e, th=th, tpar=tpar, dcm=dcm: e.tensor_copy(
                                t_dacc[th].rearrange("p (tq dd j) -> p tq dd j",
                                                     tq=4, dd=4)[:, :, dcm, :],
                                pt[tpar].rearrange("p (tq j) -> p tq j", tq=4)),
                            waits=ew, inc=True)
                        if dcm == 3:
                            do_uses[th] += 1
                            p.emit("gpsimd", lambda e, th=th, dc=dc: e.dma_start(
                                out=out_view4[th, :, :, dc // 4, :],
                                in_=t_dacc[th].rearrange("p (tq c) -> p tq c", tq=4),
                                accum_op=ALU.add),
                                waits=[(V, dacc_lastV[th]), (sO0, o0_done),
                                       (sAG, ag_total)],
                                inc=sDO[th], inc_by=16)

                dtoks = {}

                def dw_slab(dc):
                    par = dc % 2
                    xw = [(T, dw_freeT[par])]
                    if dc < 2:
                        xw = [(T, gw_freeT[0]), (T, gw_freeT[1]),
                              (sGW[0], p.cnt[sGW[0]]), (sGW[1], p.cnt[sGW[1]]),
                              (sUW[0], p.cnt[sUW[0]]), (sUW[1], p.cnt[sUW[1]]),
                              (V, p.cnt[V]), (A, p.cnt[A])]
                    dtoks[dc] = p.dma("gpsimd", t_dwb[par][:], dw[h, dc], sDW[par],
                                      waits=xw)

                dw_slab(0)
                for dc in range(DC + 2):
                    if dc < DC:
                        par = dc % 2
                        if dc + 1 < DC:
                            dw_slab(dc + 1)
                        if dc >= 2:
                            down_epilogue(dc - 2)
                        w = [(sDW[par], dtoks[dc]), (V, pd_freeV[par])]
                        for c in range(IC_H):
                            for th in range(TH):
                                p.emit("tensor", lambda e, par=par, th=th, c=c: e.matmul(
                                    pd[par][th][:], t_dwb[par][:, c * 128:(c + 1) * 128],
                                    t_mid[:, c * R + th * TM: c * R + (th + 1) * TM],
                                    start=(c == 0), stop=(c == IC_H - 1)), waits=w)
                                w = []
                        t_dns[dc] = t_dn = p.milestone("tensor")
                        dw_freeT[par] = t_dn
                    else:
                        down_epilogue(dc - 2)

    p.wait_only("gpsimd", [(sDO[0], 16 * do_uses[0]), (sDO[1], 16 * do_uses[1]),
                           (sAG, p.cnt[sAG]), (sO0, o0_done)])
    p.finalize()
    return nc, stack


def make_consts():
    k_all = np.arange(KC * 128)
    l_all = np.arange(L)
    kl = (np.outer(l_all, k_all) % L).astype(np.float64) * (2 * np.pi / L)
    cosm = np.cos(kl)
    sinm = -np.sin(kl)
    cosm[:, 2049:] = 0.0
    sinm[:, 2049:] = 0.0

    def to_slabs(mat):
        m4 = mat.reshape(32, 128, KC, 128)
        m5 = m4.transpose(2, 0, 1, 3).reshape(KC, 4, 8, 128, 128)
        return m5.transpose(0, 1, 3, 2, 4)
    cs = to_slabs(cosm)
    sn = to_slabs(sinm)
    csq = np.stack([cs, sn], axis=3)
    csq = np.ascontiguousarray(csq.reshape(KC, 4, 128, 2048)).astype(BF16NP)

    wk = np.full(KC * 128, 2.0)
    wk[0] = 1.0
    wk[2048] = 1.0
    wk[2049:] = 0.0
    tau = np.arange(TAU)
    kt = (np.outer(k_all, tau) % L).astype(np.float64) * (2 * np.pi / L)
    dm = np.cos(kt) * (wk / L)[:, None]
    dm[:, 2049:] = 0.0
    dm = dm.reshape(KC, 128, TAU).astype(BF16NP)
    return csq, np.ascontiguousarray(dm)


_CONSTS = None


def prep_inputs(L0_attn_x, L0_ffn_gate_w, L0_ffn_gate_s, L0_ffn_up_w, L0_ffn_up_s,
                L0_ffn_down_w, L0_ffn_down_s, R):
    global _CONSTS
    if _CONSTS is None:
        _CONSTS = make_consts()
    csq, dm = _CONSTS
    x = np.ascontiguousarray(np.asarray(L0_attn_x, np.float32).reshape(-1, L))
    gwt = np.ascontiguousarray(np.asarray(L0_ffn_gate_w, np.int32)
                               .reshape(2, IC_H, 128, DC, 128).transpose(0, 1, 4, 3, 2))
    uwt = np.ascontiguousarray(np.asarray(L0_ffn_up_w, np.int32)
                               .reshape(2, IC_H, 128, DC, 128).transpose(0, 1, 4, 3, 2))
    dwt = np.ascontiguousarray(np.asarray(L0_ffn_down_w, np.int32)
                               .reshape(DC, 128, 2, IC_H, 128).transpose(2, 0, 4, 3, 1))
    scl = np.zeros((128, 4), np.float32)
    scl[:, 0] = np.float32(np.asarray(L0_ffn_gate_s).reshape(-1)[0])
    scl[:, 1] = np.float32(np.asarray(L0_ffn_up_s).reshape(-1)[0])
    scl[:, 2] = np.float32(np.asarray(L0_ffn_down_s).reshape(-1)[0])
    common = {
        "gw": gwt, "uw": uwt, "dw": dwt, "csq": csq, "dmat": dm, "scl": scl,
        "idbf": np.eye(128, dtype=BF16NP), "idf": np.eye(128, dtype=np.float32),
    }
    ncores = x.shape[0] // R
    in_maps = [dict(common, x=np.ascontiguousarray(x[c * R:(c + 1) * R]))
               for c in range(ncores)]
    return in_maps


def kernel(L0_attn_x, L0_ffn_gate_w, L0_ffn_gate_s, L0_ffn_up_w, L0_ffn_up_s,
           L0_ffn_down_w, L0_ffn_down_s):
    R = 1024
    in_maps = prep_inputs(L0_attn_x, L0_ffn_gate_w, L0_ffn_gate_s, L0_ffn_up_w,
                          L0_ffn_up_s, L0_ffn_down_w, L0_ffn_down_s, R)
    nc, stack = build_program(R)
    try:
        res = run_bass_kernel_spmd(nc, in_maps, core_ids=list(range(NCORES)))
    finally:
        stack.close()
    out = np.concatenate([np.asarray(r["out"], np.float32) for r in res.results], axis=0)
    return out.reshape(2, L, L)
